# version 20 (final)
# baseline (speedup 1.0000x reference)
import numpy as np
from contextlib import ExitStack

import concourse.bass as bass
import concourse.mybir as mybir
from concourse.bass_utils import run_bass_kernel_spmd

F32 = mybir.dt.float32
BF16 = mybir.dt.bfloat16
I32 = mybir.dt.int32
A = mybir.AluOpType
AF = mybir.ActivationFunctionType
AX = mybir.AxisListType

P = 128
D = 1024
KT = D // P
C = 256
MT = C // P
T = 8192
NCH = T // C
INV_D = 1.0 / D
QMAX_INV = 1.0 / 127.0
LN_EPS = 1e-5
S_NEGM, S_INV, S_SSQ, S_XS, S_REC, S_SUM, S_LNT = 0, 2, 4, 6, 8, 10, 12


class Counts:

    def __init__(self, nc):
        self.nc = nc
        self.val = {}
        self.last_wait = {}

    def sem(self, name, es):
        s = es.enter_context(self.nc.semaphore(name))
        self.val[name] = 0
        setattr(self, name, s)

    def inc(self, instr, name, n=1):
        instr.then_inc(getattr(self, name), n)
        self.val[name] += n
        return self.val[name]

    def wait(self, eng, ekey, sname, value=None):
        v = self.val[sname] if value is None else value
        if v <= 0:
            return
        key = (ekey, sname)
        if self.last_wait.get(key, -1) >= v:
            return
        eng.wait_ge(getattr(self, sname), v)
        self.last_wait[key] = v


def build_nc(nch=NCH, taps=False):
    nc = bass.Bass("TRN2", target_bir_lowering=False)

    xin = nc.dram_tensor("xin", [T, D], F32, kind="ExternalInput").ap()
    w_hbm = {nm: nc.dram_tensor(nm, [D, D], F32, kind="ExternalInput").ap()
             for nm in ["wq", "wk", "wv", "wo"]}
    wf_hbm = nc.dram_tensor("wf", [D, D], F32, kind="ExternalInput").ap()
    b_hbm = {nm: nc.dram_tensor(nm, [D], F32, kind="ExternalInput").ap()
             for nm in ["bq", "bk", "bv", "bo", "bf"]}
    p_hbm = nc.dram_tensor("decay", [1, 1], F32, kind="ExternalInput").ap()
    iota_bc_h = nc.dram_tensor("iota_bc", [P, C], F32, kind="ExternalInput").ap()
    iota_rev_h = nc.dram_tensor("iota_rev", [P, MT], F32, kind="ExternalInput").ap()
    expo_h = nc.dram_tensor("expo", [P, MT * C], F32, kind="ExternalInput").ap()
    binm_h = nc.dram_tensor("binm", [P, MT * C], F32, kind="ExternalInput").ap()
    ident_h = nc.dram_tensor("ident", [P, P], F32, kind="ExternalInput").ap()
    yout = nc.dram_tensor("yout", [T, D], F32, kind="ExternalOutput").ap()
    tap_specs = [("xn1b", [P, MT * D]), ("qT", [P, KT * C]), ("qTs", [P, KT * C]),
                 ("k_tm", [P, MT * D]), ("kTb", [P, KT * C]), ("v_tm", [P, MT * D]),
                 ("at_b", [P, MT * C]), ("rdT", [P, KT * C]), ("S_f", [P, KT * D]),
                 ("xqb", [P, D]), ("maskt", [P, MT * C]), ("lp_bc", [P, C]),
                 ("STt", [P, 16])]
    dbg = {}
    if taps:
        for nm, shape in tap_specs:
            dbg[nm] = nc.dram_tensor("dbg_" + nm, shape, F32, kind="ExternalOutput").ap()

    with ExitStack() as es:
        def sb(name, shape, dtype, **kw):
            return es.enter_context(nc.sbuf_tensor(name, shape, dtype, **kw))

        cnt = Counts(nc)
        for s in ["ld", "tr", "st", "pe", "v", "s"]:
            cnt.sem(s, es)

        def twait(sn, v=None):
            cnt.wait(nc.tensor, "t", sn, v)

        def vwait(sn, v=None):
            cnt.wait(nc.vector, "v", sn, v)

        def swait(sn, v=None):
            cnt.wait(nc.scalar, "s", sn, v)

        def gwait(sn, v=None):
            cnt.wait(nc.gpsimd, "g", sn, v)

        def ywait(sn, v=None):
            cnt.wait(nc.sync, "y", sn, v)

        def vflush():
            cnt.wait(nc.vector, "v", "v")

        def sflush():
            cnt.wait(nc.scalar, "s", "s")

        ws = {nm: sb(nm + "_s", [P, KT * D], BF16) for nm in ["wq", "wk", "wv", "wo"]}
        wfT = sb("wfT_s", [P, KT * D], BF16)
        S_f = sb("S_f", [P, KT * D], F32)
        S_b = sb("S_b", [P, KT * D], BF16)
        bq_pt = sb("bq_pt", [P, KT], F32)
        brows = sb("brows", [P, D], BF16)
        brow = {"bk": brows[0:1, :], "bv": brows[32:33, :], "bo": brows[64:65, :]}
        ones_row = sb("ones_row", [P, C], BF16)
        ones_f32 = sb("ones_f32", [1, P], F32)
        wsc_bc = sb("wsc_bc", [P, D], F32)
        bf_bc = sb("bf_bc", [P, D], F32)
        maskt = sb("maskt", [P, MT * C], BF16)
        lp_bc = sb("lp_bc", [P, C], BF16)
        lrev = sb("lrev", [P, MT], F32)
        lamC = sb("lamC", [P, 1], F32)
        logl = sb("logl", [P, 2], F32)
        STt = sb("stats", [P, 16], F32)
        eps_t = sb("eps_t", [P, 1], F32)

        psA = es.enter_context(nc.psum_tensor("psA", [P, 512], F32))
        psB = es.enter_context(nc.psum_tensor("psB", [P, 512], F32))
        psC = es.enter_context(nc.psum_tensor("psC", [P, 512], F32))
        psD = es.enter_context(nc.psum_tensor("psD", [P, 512], F32))
        psE = es.enter_context(nc.psum_tensor("psE", [P, 1024], F32))
        psF = es.enter_context(nc.psum_tensor("psF", [P, 1024], F32))
        bank_guard = {}

        def acquire(bname):
            g = bank_guard.get(bname)
            if g is not None:
                twait(g[0], g[1])

        def release(bname, sname):
            bank_guard[bname] = (sname, cnt.val[sname])

        for nm in ["wq", "wk", "wv", "wo"]:
            cnt.inc(nc.gpsimd.dma_start(
                ws[nm][:, :].rearrange("p (a n) -> p a n", a=KT),
                w_hbm[nm].rearrange("(a p) n -> p a n", p=P)), "ld", 16)
        cnt.inc(nc.gpsimd.dma_start(
            bq_pt[:, :], b_hbm["bq"].rearrange("(a p) -> p a", p=P),
            allow_slow_non_contiguous=True), "ld", 16)
        for nm in ["bk", "bv", "bo"]:
            cnt.inc(nc.gpsimd.dma_start(brow[nm], b_hbm[nm][None, :]), "ld", 16)

        with ExitStack() as es2:
            def sb2(name, shape, dtype, **kw):
                return es2.enter_context(nc.sbuf_tensor(name, shape, dtype, **kw))
            p_t = sb2("p_t", [1, 1], F32)
            iota_bc = sb2("iota_bc_s", [P, C], F32)
            iota_rev = sb2("iota_rev_s", [P, MT], F32)
            expo = sb2("expo_s", [P, MT * C], F32)
            binm = sb2("binm_s", [P, MT * C], F32)
            ident = sb2("ident_s", [P, P], F32)
            bfrow_f = sb2("bfrow_f", [1, D], F32)
            wf_st = sb2("wf_st", [P, D], F32)
            wfq_b = sb2("wfq_b", [P, KT * D], BF16)
            wsq = sb2("wsq", [P, KT], F32)
            wrec = sb2("wrec", [P, KT], F32)
            wqi = sb2("wqi", [P, D], I32)
            g0s = sb2("g0s", [P, D], F32)
            g1s = sb2("g1s", [P, MT * C], F32)

            for ap_, h in [(p_t, p_hbm), (iota_bc, iota_bc_h), (iota_rev, iota_rev_h),
                           (expo, expo_h), (binm, binm_h), (ident, ident_h),
                           (bfrow_f, b_hbm["bf"][None, :])]:
                cnt.inc(nc.sync.dma_start(ap_[:, :], h), "ld", 16)
            ld_consts = cnt.val["ld"]

            cnt.inc(nc.vector.memset(ones_row[:, :], 1.0), "v")
            cnt.inc(nc.vector.memset(ones_f32[:, :], 1.0), "v")
            cnt.inc(nc.vector.memset(eps_t[:, :], LN_EPS), "v")
            cnt.inc(nc.vector.memset(S_f[:, :], 0.0), "v")
            cnt.inc(nc.vector.memset(S_b[:, :], 0.0), "v")

            twait("ld", ld_consts)
            cnt.inc(nc.tensor.matmul(psA[:, :1], ones_f32[:1, :], p_t[:1, :1],
                                     start=True, stop=True), "pe")
            swait("pe")
            cnt.inc(nc.scalar.activation(logl[:, 1:2], psA[:, :1], AF.Sigmoid), "s")
            cnt.inc(nc.scalar.activation(logl[:, 0:1], logl[:, 1:2], AF.Ln), "s")
            vwait("s")
            cnt.inc(nc.vector.tensor_scalar(
                logl[:, 0:1], logl[:, 0:1], -1e30, None, A.max), "v")
            swait("v")
            cnt.inc(nc.scalar.activation(lp_bc[:, :], iota_bc[:, :], AF.Exp,
                                         scale=logl[:, :1]), "s")
            cnt.inc(nc.scalar.activation(lrev[:, :], iota_rev[:, :], AF.Exp,
                                         scale=logl[:, :1]), "s")
            cnt.inc(nc.scalar.activation(lamC[:, :], logl[:, :1], AF.Exp,
                                         scale=float(C)), "s")
            vwait("s")
            cnt.inc(nc.vector.tensor_scalar(g0s[:, :MT * C], expo[:, :], logl[:, :1],
                                            0.0, A.mult, A.min), "v")
            swait("v")
            cnt.inc(nc.scalar.activation(g1s[:, :], g0s[:, :MT * C], AF.Exp), "s")
            vwait("s")
            cnt.inc(nc.vector.tensor_tensor(
                maskt[:, :], g1s[:, :], binm[:, :], A.mult), "v")

            for a in range(KT):
                ywait("v")
                cnt.inc(nc.sync.dma_start(
                    wf_st[:, :], wf_hbm[a * P:(a + 1) * P, :]), "ld", 16)
                vwait("ld")
                cnt.inc(nc.vector.tensor_reduce(
                    wsq[:, a:a + 1], wf_st[:, :], AX.X, A.max,
                    apply_absolute_value=True), "v")
                cnt.wait(nc.vector, "v", "v")
                cnt.inc(nc.vector.tensor_scalar(
                    wsq[:, a:a + 1], wsq[:, a:a + 1], 1e-8, QMAX_INV,
                    A.max, A.mult), "v")
                cnt.wait(nc.vector, "v", "v")
                cnt.inc(nc.vector.reciprocal(wrec[:, a:a + 1], wsq[:, a:a + 1]), "v")
                cnt.wait(nc.vector, "v", "v")
                cnt.inc(nc.vector.tensor_scalar(
                    wqi[:, :], wf_st[:, :], wrec[:, a:a + 1], None, A.mult), "v")
                cnt.inc(nc.vector.tensor_copy(
                    wfq_b[:, a * D:(a + 1) * D], wqi[:, :]), "v")
            v_wfq = cnt.val["v"]

            ywait("v", v_wfq)
            for a in range(KT):
                out_ap = bass.AP(wfT, a * P, [[KT * D, P], [D, KT], [1, P]])
                cnt.inc(nc.sync.dma_start_transpose(
                    out_ap, wfq_b[:, a * D:(a + 1) * D]), "trw", 16)

            twait("v", v_wfq)
            for a in range(KT):
                twait("v")
                cnt.inc(nc.tensor.transpose(
                    psA[:1, :P], wsq[:, a:a + 1], ident[:, :]), "pe")
                vwait("pe")
                cnt.inc(nc.vector.tensor_copy(
                    g0s[:1, a * P:(a + 1) * P], psA[:1, :P]), "v")
            for h in range(2):
                twait("v")
                cnt.inc(nc.tensor.matmul(psA[:, :], ones_f32[:1, :],
                                         g0s[:1, h * 512:(h + 1) * 512],
                                         start=True, stop=True), "pe")
                vwait("pe")
                cnt.inc(nc.vector.tensor_copy(
                    wsc_bc[:, h * 512:(h + 1) * 512], psA[:, :]), "v")
            for h in range(2):
                twait("v")
                cnt.inc(nc.tensor.matmul(psA[:, :], ones_f32[:1, :],
                                         bfrow_f[:1, h * 512:(h + 1) * 512],
                                         start=True, stop=True), "pe")
                vwait("pe")
                cnt.inc(nc.vector.tensor_copy(
                    bf_bc[:, h * 512:(h + 1) * 512], psA[:, :]), "v")
            cnt.inc(nc.vector.tensor_reduce(
                sbf_t[:, :], bf_bc[:, :], AX.X, A.add), "v")
            vflush()

        setup = {k: cnt.val[k] for k in ["v", "s", "trw", "ld", "pe"]}

        xb = [sb(f"xb{i}", [P, MT * D], F32) for i in range(2)]
        xn1b = sb("xn1b", [P, MT * D], BF16)
        xn1T = sb("xn1T", [P, KT * C], BF16, is_transpose_dest=True)
        qT = sb("qT", [P, KT * C], BF16)
        qTs = sb("qTs", [P, KT * C], BF16)
        k_tm = sb("k_tm", [P, MT * D], BF16)
        kTb = sb("kTb", [P, KT * C], BF16, is_transpose_dest=True)
        v_tm = sb("v_tm", [P, MT * D], BF16)
        at_b = sb("at_b", [P, MT * C], BF16)
        rdT = sb("rdT", [P, KT * C], BF16)
        G0 = sb("G0", [P, D], F32)
        G1 = sb("G1", [P, D], F32)
        I1 = sb("I1", [P, 512], I32)
        xqb = sb("xqb", [P, D], BF16)
        xqT = sb("xqT", [P, D], BF16, is_transpose_dest=True)

        for sn, val in setup.items():
            vwait(sn, val)
            swait(sn, val)
            ywait(sn, val)
            twait(sn, val)
            gwait(sn, val)

        def load_x(c, st_req=0):
            cnt.wait(nc.scalar, "s", "st", st_req)
            cnt.inc(nc.scalar.dma_start(
                xb[c % 2][:, :].rearrange("p (a d) -> p a d", a=MT),
                xin[c * C:(c + 1) * C, :].rearrange("(a p) d -> p a d", p=P)),
                "ld", 16)

        load_x(0)
        st_guard = 0
        s_sb_prev = 0

        def ln_stats(src_ap, junk_ap, mt):
            cnt.inc(nc.vector.tensor_reduce(
                STt[:, S_SUM + mt:S_SUM + mt + 1], src_ap, AX.X, A.add), "v")
            vflush()
            cnt.inc(nc.vector.tensor_scalar(
                STt[:, S_NEGM + mt:S_NEGM + mt + 1],
                STt[:, S_SUM + mt:S_SUM + mt + 1], -INV_D, None, A.mult), "v")
            swait("v")
            cnt.inc(nc.scalar.activation(
                junk_ap, src_ap,
                AF.Square, bias=STt[:, S_NEGM + mt:S_NEGM + mt + 1],
                accum_out=STt[:, S_SSQ + mt:S_SSQ + mt + 1]), "s")
            sflush()
            cnt.inc(nc.scalar.activation(
                STt[:, S_LNT + mt:S_LNT + mt + 1],
                STt[:, S_SSQ + mt:S_SSQ + mt + 1], AF.Ln,
                scale=INV_D, bias=eps_t[:, :1]), "s")
            cnt.inc(nc.scalar.activation(
                STt[:, S_INV + mt:S_INV + mt + 1],
                STt[:, S_LNT + mt:S_LNT + mt + 1], AF.Exp, scale=-0.5), "s")
            vwait("s")

        for c in range(nch):
            xc = xb[c % 2]
            x_ld = setup["ld"] + 16 * (c + 1)
            if c + 1 < nch:
                load_x(c + 1, st_guard)

            vwait("ld", x_ld)
            swait("ld", x_ld)
            for mt in range(MT):
                ln_stats(xc[:, mt * D:(mt + 1) * D], G1[:, :], mt)
                cnt.inc(nc.vector.tensor_scalar(
                    xn1b[:, mt * D:(mt + 1) * D], xc[:, mt * D:(mt + 1) * D],
                    STt[:, S_NEGM + mt:S_NEGM + mt + 1],
                    STt[:, S_INV + mt:S_INV + mt + 1], A.add, A.mult), "v")
            v_xn1 = cnt.val["v"]

            ywait("v", v_xn1)
            for mt in range(MT):
                out_ap = bass.AP(xn1T, mt * P, [[KT * C, P], [C, KT], [1, P]])
                cnt.inc(nc.sync.dma_start_transpose(
                    out_ap, xn1b[:, mt * D:(mt + 1) * D]), "tr", 16)
            tr_xn1 = cnt.val["tr"]

            twait("tr", tr_xn1)
            for g in range(4):
                bank, bname = (psA, "A") if g % 2 == 0 else (psB, "B")
                acquire(bname)
                for mi in range(2):
                    m = 2 * g + mi
                    for kk in range(KT):
                        mm = nc.tensor.matmul(
                            bank[:, mi * C:(mi + 1) * C],
                            ws["wq"][:, kk * D + m * P: kk * D + (m + 1) * P],
                            xn1T[:, kk * C:(kk + 1) * C],
                            start=(kk == 0), stop=(kk == KT - 1))
                cnt.inc(mm, "pe")
                swait("pe")
                for mi in range(2):
                    m = 2 * g + mi
                    cnt.inc(nc.scalar.activation(
                        qT[:, m * C:(m + 1) * C], bank[:, mi * C:(mi + 1) * C],
                        AF.Identity, bias=bq_pt[:, m:m + 1]), "s")
                release(bname, "s")
            s_qev = cnt.val["s"]
            vwait("s", s_qev)
            for kk in range(KT):
                cnt.inc(nc.vector.tensor_tensor(
                    qTs[:, kk * C:(kk + 1) * C], qT[:, kk * C:(kk + 1) * C],
                    lp_bc[:, :], A.mult), "v")

            for mt in range(MT):
                bank, bname = (psE, "E") if mt == 0 else (psF, "F")
                acquire(bname)
                for h in range(2):
                    for kk in range(KT):
                        mm = nc.tensor.matmul(
                            bank[:, h * 512:(h + 1) * 512],
                            xn1T[:, kk * C + mt * P: kk * C + (mt + 1) * P],
                            ws["wk"][:, kk * D + h * 512: kk * D + (h + 1) * 512],
                            start=(kk == 0), stop=False)
                    mm = nc.tensor.matmul(
                        bank[:, h * 512:(h + 1) * 512], ones_row[:1, :P],
                        brows[0:1, h * 512:(h + 1) * 512],
                        start=False, stop=True)
                cnt.inc(mm, "pe")
                vwait("pe")
                vwait("s")
                cnt.inc(nc.vector.tensor_scalar(
                    G0[:, :], bank[:, :], 0.0, None, A.min), "v")
                swait("v")
                cnt.inc(nc.scalar.activation(G1[:, :], G0[:, :], AF.Exp), "s")
                vwait("s")
                cnt.inc(nc.vector.scalar_tensor_tensor(
                    k_tm[:, mt * D:(mt + 1) * D], bank[:, :], 0.0,
                    G1[:, :], A.max, A.add), "v")
                release(bname, "v")
            v_ktm = cnt.val["v"]

            ywait("v", v_ktm)
            for mt in range(MT):
                out_ap = bass.AP(kTb, mt * P, [[KT * C, P], [C, KT], [1, P]])
                cnt.inc(nc.sync.dma_start_transpose(
                    out_ap, k_tm[:, mt * D:(mt + 1) * D]), "tr", 16)
            tr_k = cnt.val["tr"]
            vwait("tr", tr_k)
            for mt in range(MT):
                cnt.inc(nc.vector.tensor_scalar(
                    k_tm[:, mt * D:(mt + 1) * D], k_tm[:, mt * D:(mt + 1) * D],
                    lrev[:, mt:mt + 1], None, A.mult), "v")
            v_klam = cnt.val["v"]

            for i, (bank, bname) in enumerate(
                    [(psA, "A"), (psB, "B"), (psC, "C"), (psD, "D")]):
                mt, h = divmod(i, 2)
                acquire(bname)
                for kk in range(KT):
                    mm = nc.tensor.matmul(
                        bank[:, :],
                        xn1T[:, kk * C + mt * P: kk * C + (mt + 1) * P],
                        ws["wv"][:, kk * D + h * 512: kk * D + (h + 1) * 512],
                        start=(kk == 0), stop=False)
                mm = nc.tensor.matmul(
                    bank[:, :], ones_row[:1, :P],
                    brows[32:33, h * 512:(h + 1) * 512], start=False, stop=True)
                cnt.inc(mm, "pe")
                vwait("pe")
                cnt.inc(nc.vector.tensor_copy(
                    v_tm[:, mt * D + h * 512: mt * D + (h + 1) * 512],
                    bank[:, :]), "v")
                release(bname, "v")
            v_vtm = cnt.val["v"]

            acquire("A")
            twait("tr", tr_k)
            twait("s", s_qev)
            for mtk in range(MT):
                for kk in range(KT):
                    mm = nc.tensor.matmul(
                        psA[:, mtk * 256:(mtk + 1) * 256],
                        kTb[:, kk * C + mtk * P: kk * C + (mtk + 1) * P],
                        qT[:, kk * C:(kk + 1) * C],
                        start=(kk == 0), stop=(kk == KT - 1))
            cnt.inc(mm, "pe")
            vwait("pe")
            cnt.inc(nc.vector.tensor_tensor(
                at_b[:, :], psA[:, :], maskt[:, :], A.mult), "v")
            release("A", "v")
            v_atb = cnt.val["v"]

            twait("v", v_atb)
            twait("s", s_sb_prev)
            for gi, (bank, bname) in enumerate(
                    [(psB, "B"), (psC, "C"), (psD, "D"), (psE, "E")]):
                acquire(bname)
                for half in range(2):
                    j = gi * 2 + half
                    tgt = bank[:, half * C:(half + 1) * C]
                    for a in range(MT):
                        mm = nc.tensor.matmul(
                            tgt, v_tm[:, a * D + j * P: a * D + (j + 1) * P],
                            at_b[:, a * C:(a + 1) * C],
                            start=(a == 0), stop=False)
                    for kk in range(KT):
                        mm = nc.tensor.matmul(
                            tgt, S_b[:, kk * D + j * P: kk * D + (j + 1) * P],
                            qTs[:, kk * C:(kk + 1) * C],
                            start=False, stop=(kk == KT - 1))
                cnt.inc(mm, "pe")
                swait("pe")
                for half in range(2):
                    j = gi * 2 + half
                    cnt.inc(nc.scalar.activation(
                        rdT[:, j * C:(j + 1) * C],
                        bank[:, half * C:(half + 1) * C], AF.Copy), "s")
                release(bname, "s")
            s_rdT = cnt.val["s"]

            twait("v", v_klam)
            vwait("s", s_sb_prev)
            for i in range(KT):
                bank, bname = (psF, "F") if i % 2 == 0 else (psE, "E")
                acquire(bname)
                for h in range(2):
                    for a in range(MT):
                        mm = nc.tensor.matmul(
                            bank[:, h * 512:(h + 1) * 512],
                            k_tm[:, a * D + i * P: a * D + (i + 1) * P],
                            v_tm[:, a * D + h * 512: a * D + (h + 1) * 512],
                            start=(a == 0), stop=(a == MT - 1))
                cnt.inc(mm, "pe")
                vwait("pe")
                cnt.inc(nc.vector.scalar_tensor_tensor(
                    S_f[:, i * D:(i + 1) * D], S_f[:, i * D:(i + 1) * D],
                    lamC[:, :1], bank[:, :], A.mult, A.add), "v")
                release(bname, "v")
            v_supd = cnt.val["v"]
            swait("v", v_supd)
            for i in range(KT):
                cnt.inc(nc.scalar.activation(
                    S_b[:, i * D:(i + 1) * D], S_f[:, i * D:(i + 1) * D],
                    AF.Copy), "s")
            s_sb_prev = cnt.val["s"]

            twait("s", s_rdT)
            for mt in range(MT):
                bank, bname = (psF, "F") if mt == 0 else (psE, "E")
                acquire(bname)
                for h in range(2):
                    for kk in range(KT):
                        mm = nc.tensor.matmul(
                            bank[:, h * 512:(h + 1) * 512],
                            rdT[:, kk * C + mt * P: kk * C + (mt + 1) * P],
                            ws["wo"][:, kk * D + h * 512: kk * D + (h + 1) * 512],
                            start=(kk == 0), stop=False)
                    mm = nc.tensor.matmul(
                        bank[:, h * 512:(h + 1) * 512], ones_row[:1, :P],
                        brows[64:65, h * 512:(h + 1) * 512],
                        start=False, stop=True)
                cnt.inc(mm, "pe")
                vwait("pe")
                vwait("s")
                cnt.inc(nc.vector.tensor_copy(G0[:, :], bank[:, :]), "v")
                release(bname, "v")
                ln_stats(G0[:, :], G1[:, :], mt)
                cnt.inc(nc.vector.tensor_scalar(
                    G1[:, :], G0[:, :],
                    STt[:, S_NEGM + mt:S_NEGM + mt + 1],
                    STt[:, S_INV + mt:S_INV + mt + 1], A.add, A.mult), "v")
                cnt.inc(nc.vector.tensor_tensor(
                    xc[:, mt * D:(mt + 1) * D], G1[:, :],
                    xc[:, mt * D:(mt + 1) * D], A.add), "v")

            tr_base = cnt.val["tr"]
            for mt in range(MT):
                bank, bname = (psF, "F") if mt == 0 else (psE, "E")
                vwait("s")
                ln_stats(xc[:, mt * D:(mt + 1) * D], G1[:, :], mt)
                cnt.inc(nc.vector.tensor_scalar(
                    G0[:, :], xc[:, mt * D:(mt + 1) * D],
                    STt[:, S_NEGM + mt:S_NEGM + mt + 1],
                    STt[:, S_INV + mt:S_INV + mt + 1], A.add, A.mult), "v")
                cnt.inc(nc.vector.tensor_reduce(
                    STt[:, S_XS + mt:S_XS + mt + 1], G0[:, :],
                    AX.X, A.max, apply_absolute_value=True), "v")
                vflush()
                cnt.inc(nc.vector.tensor_scalar(
                    STt[:, S_XS + mt:S_XS + mt + 1],
                    STt[:, S_XS + mt:S_XS + mt + 1], 1e-8, QMAX_INV,
                    A.max, A.mult), "v")
                vflush()
                cnt.inc(nc.vector.reciprocal(
                    STt[:, S_REC + mt:S_REC + mt + 1],
                    STt[:, S_XS + mt:S_XS + mt + 1]), "v")
                vflush()
                vwait("tr", tr_base + 16 * mt)
                for h in range(2):
                    cnt.inc(nc.vector.tensor_scalar(
                        I1[:, :], G0[:, h * 512:(h + 1) * 512],
                        STt[:, S_REC + mt:S_REC + mt + 1], None, A.mult), "v")
                    cnt.inc(nc.vector.tensor_copy(
                        xqb[:, h * 512:(h + 1) * 512], I1[:, :]), "v")
                v_xq = cnt.val["v"]
                ywait("v", v_xq)
                out_ap = bass.AP(xqT, 0, [[D, P], [P, KT], [1, P]])
                cnt.inc(nc.sync.dma_start_transpose(out_ap, xqb[:, :]), "tr", 16)
                tr_xq = cnt.val["tr"]

                twait("tr", tr_xq)
                acquire(bname)
                for h in range(2):
                    for kk in range(KT):
                        mm = nc.tensor.matmul(
                            bank[:, h * 512:(h + 1) * 512],
                            xqT[:, kk * P:(kk + 1) * P],
                            wfT[:, kk * D + h * 512: kk * D + (h + 1) * 512],
                            start=(kk == 0), stop=(kk == KT - 1))
                cnt.inc(mm, "pe")
                vwait("pe")
                cnt.inc(nc.vector.scalar_tensor_tensor(
                    G1[:, :], bank[:, :],
                    STt[:, S_XS + mt:S_XS + mt + 1], wsc_bc[:, :],
                    A.mult, A.mult), "v")
                release(bname, "v")
                cnt.inc(nc.vector.tensor_tensor(
                    G0[:, :], G1[:, :], bf_bc[:, :], A.add), "v")
                ln_stats(G0[:, :], G1[:, :], mt)
                cnt.inc(nc.vector.tensor_scalar(
                    G1[:, :], G0[:, :],
                    STt[:, S_NEGM + mt:S_NEGM + mt + 1],
                    STt[:, S_INV + mt:S_INV + mt + 1], A.add, A.mult), "v")
                cnt.inc(nc.vector.tensor_tensor(
                    xc[:, mt * D:(mt + 1) * D], G1[:, :],
                    xc[:, mt * D:(mt + 1) * D], A.add), "v")
                v_out = cnt.val["v"]
                swait("v", v_out)
                cnt.inc(nc.scalar.dma_start(
                    yout[c * C + mt * P: c * C + (mt + 1) * P, :],
                    xc[:, mt * D:(mt + 1) * D]), "st", 16)
            st_guard = cnt.val["st"]

        if taps:
            local = {"xn1b": xn1b, "qT": qT, "qTs": qTs, "k_tm": k_tm,
                     "kTb": kTb, "v_tm": v_tm, "at_b": at_b, "rdT": rdT,
                     "S_f": S_f, "xqb": xqb, "maskt": maskt, "lp_bc": lp_bc,
                     "STt": STt}
            ywait("v")
            ywait("s")
            for nm, ap_h in dbg.items():
                src = local[nm]
                wtot = ap_h.shape[1]
                for off in range(0, wtot, D):
                    wd = min(D, wtot - off)
                    ywait("v")
                    cnt.inc(nc.vector.tensor_copy(G0[:, :wd],
                                                  src[:, off:off + wd]), "v")
                    ywait("v")
                    cnt.inc(nc.sync.dma_start(ap_h[:, off:off + wd],
                                              G0[:, :wd]), "st", 16)
                    ywait("st")
                    cnt.wait(nc.vector, "v", "st", cnt.val["st"])
        nc.sync.wait_ge(cnt.st, cnt.val["st"])
    return nc


def _consts():
    iota_bc = np.broadcast_to(
        np.arange(C, dtype=np.float32)[None, :], (P, C)).copy()
    iota_rev = np.empty((P, MT), np.float32)
    for mt in range(MT):
        iota_rev[:, mt] = C - 1 - (mt * P + np.arange(P))
    expo = np.empty((P, MT * C), np.float32)
    binm = np.empty((P, MT * C), np.float32)
    for a in range(MT):
        j = a * P + np.arange(P)[:, None]
        i = np.arange(C)[None, :]
        expo[:, a * C:(a + 1) * C] = (i - 1 - j).astype(np.float32)
        binm[:, a * C:(a + 1) * C] = (j < i).astype(np.float32)
    ident = np.eye(P, dtype=np.float32)
    return iota_bc, iota_rev, expo, binm, ident


def kernel_v1(L0_attn_x, L0_attn_Wq, L0_attn_bq, L0_attn_Wk, L0_attn_bk,
           L0_attn_Wv, L0_attn_bv, L0_attn_Wo, L0_attn_bo,
           L0_attn_decay_param, L0_ffn_weight, L0_ffn_bias):
    x = np.ascontiguousarray(np.asarray(L0_attn_x, dtype=np.float32))
    B = x.shape[0]
    iota_bc, iota_rev, expo, binm, ident = _consts()
    common = {
        "wq": np.ascontiguousarray(L0_attn_Wq, dtype=np.float32),
        "wk": np.ascontiguousarray(L0_attn_Wk, dtype=np.float32),
        "wv": np.ascontiguousarray(L0_attn_Wv, dtype=np.float32),
        "wo": np.ascontiguousarray(L0_attn_Wo, dtype=np.float32),
        "wf": np.ascontiguousarray(L0_ffn_weight, dtype=np.float32),
        "bq": np.ascontiguousarray(L0_attn_bq, dtype=np.float32),
        "bk": np.ascontiguousarray(L0_attn_bk, dtype=np.float32),
        "bv": np.ascontiguousarray(L0_attn_bv, dtype=np.float32),
        "bo": np.ascontiguousarray(L0_attn_bo, dtype=np.float32),
        "bf": np.ascontiguousarray(L0_ffn_bias, dtype=np.float32),
        "decay": np.asarray(L0_attn_decay_param, dtype=np.float32).reshape(1, 1),
        "iota_bc": iota_bc, "iota_rev": iota_rev,
        "expo": expo, "binm": binm, "ident": ident,
    }
    nc = build_nc()
    in_maps = [dict(common, xin=x[b]) for b in range(B)]
    res = run_bass_kernel_spmd(nc, in_maps, core_ids=list(range(B)))
    out = np.stack([np.asarray(res.results[b]["yout"]) for b in range(B)], axis=0)
    return out.astype(np.float32)


TH = 4096
NC2 = TH // C


def build_nc2(nb=4, nch_b=NC2, taps=False):
    nc = bass.Bass("TRN2", target_bir_lowering=False)

    xin = nc.dram_tensor("xin", [TH, D], F32, kind="ExternalInput").ap()
    w_hbm = {nm: nc.dram_tensor(nm, [D, D], F32, kind="ExternalInput").ap()
             for nm in ["wq", "wk", "wv", "wo"]}
    wf_hbm = nc.dram_tensor("wf", [D, D], F32, kind="ExternalInput").ap()
    b_hbm = {nm: nc.dram_tensor(nm, [D], F32, kind="ExternalInput").ap()
             for nm in ["bq", "bk", "bv", "bo", "bf"]}
    p_hbm = nc.dram_tensor("decay", [1, 1], F32, kind="ExternalInput").ap()
    iota_bc_h = nc.dram_tensor("iota_bc", [P, C], F32, kind="ExternalInput").ap()
    iota_rev_h = nc.dram_tensor("iota_rev", [P, MT], F32, kind="ExternalInput").ap()
    expo_h = nc.dram_tensor("expo", [P, MT * C], F32, kind="ExternalInput").ap()
    binm_h = nc.dram_tensor("binm", [P, MT * C], F32, kind="ExternalInput").ap()
    ident_h = nc.dram_tensor("ident", [P, P], F32, kind="ExternalInput").ap()
    iota_cc_h = nc.dram_tensor("iota_cc", [P, NC2], F32, kind="ExternalInput").ap()
    rflag_h = nc.dram_tensor("rflag", [P, 1], F32, kind="ExternalInput").ap()
    sflag_h = nc.dram_tensor("sflag", [P, 1], F32, kind="ExternalInput").ap()
    yout = nc.dram_tensor("yout", [TH, D], F32, kind="ExternalOutput").ap()
    dbg = {}
    if taps:
        for nm, shape in [("qTs_l", [P, KT * C]), ("at_l", [P, MT * C]),
                          ("vt_l", [P, MT * D]), ("Sc0", [P, KT * D]),
                          ("rdT", [P, KT * C]), ("M_r", [P, KT * D]),
                          ("qTg", [P, KT * C]), ("xb0", [P, MT * D]),
                          ("xb1", [P, MT * D])]:
            dbg[nm] = nc.dram_tensor("dbg_" + nm, shape, F32,
                                     kind="ExternalOutput").ap()

    sp_qTs = nc.dram_tensor("sp_qTs", [NC2 * P, KT * C], BF16, kind="Internal").ap()
    sp_at = nc.dram_tensor("sp_at", [NC2 * P, MT * C], BF16, kind="Internal").ap()
    sp_vt = nc.dram_tensor("sp_vt", [NC2 * P, MT * D], BF16, kind="Internal").ap()
    sp_S = nc.dram_tensor("sp_S", [(NC2 + 1) * P, KT * D], BF16, kind="Internal").ap()
    cc_in = nc.dram_tensor("cc_in", [P, KT * D], BF16, kind="Internal")
    cc_out = nc.dram_tensor("cc_out", [P, KT * D], BF16, kind="Internal")

    with ExitStack() as es:
        def sb(name, shape, dtype, **kw):
            return es.enter_context(nc.sbuf_tensor(name, shape, dtype, **kw))

        cnt = Counts(nc)
        for s_ in ["ld", "st", "pe", "v", "s", "cc",
                   "trx", "trk", "trq", "trw",
                   "spq", "spa", "spv", "sps", "spc",
                   "ldx0", "ldx1", "lds0", "lds1", "ldsm", "ldm2", "ldat",
                   "st0", "st1"]:
            cnt.sem(s_, es)
        LT = {}

        def twait(sn, v=None):
            cnt.wait(nc.tensor, "t", sn, v)

        def vwait(sn, v=None):
            cnt.wait(nc.vector, "v", sn, v)

        def swait(sn, v=None):
            cnt.wait(nc.scalar, "s", sn, v)

        def gwait(sn, v=None):
            cnt.wait(nc.gpsimd, "g", sn, v)

        def ywait(sn, v=None):
            cnt.wait(nc.sync, "y", sn, v)

        def vflush():
            cnt.wait(nc.vector, "v", "v")

        def sflush():
            cnt.wait(nc.scalar, "s", "s")

        ws = {nm: sb(nm + "_s", [P, KT * D], BF16) for nm in ["wq", "wk", "wv", "wo"]}
        wfT = sb("wfT_s", [P, KT * D], BF16)
        bq_pt = sb("bq_pt", [P, KT], F32)
        brows = sb("brows", [P, D], BF16)
        brow = {"bk": brows[0:1, :], "bv": brows[32:33, :], "bo": brows[64:65, :]}
        ones_row = sb("ones_row", [P, C], BF16)
        ones_f32 = sb("ones_f32", [1, P], F32)
        wsc_bc = sb("wsc_bc", [P, D], F32)
        bf_bc = sb("bf_bc", [P, D], F32)
        maskt = sb("maskt", [P, MT * C], BF16)
        lp_bc = sb("lp_bc", [P, C], BF16)
        lrev = sb("lrev", [P, MT], F32)
        lamC = sb("lamC", [P, 1], F32)
        lamcc = sb("lamcc", [P, NC2], F32)
        logl = sb("logl", [P, 2], F32)
        STt = sb("stats", [P, 16], F32)
        eps_t = sb("eps_t", [P, 1], F32)
        sflag = sb("sflag_t", [P, 1], F32)
        sbf_t = sb("sbf_t", [P, 1], F32)

        psA = es.enter_context(nc.psum_tensor("psA", [P, 512], F32))
        psB = es.enter_context(nc.psum_tensor("psB", [P, 512], F32))
        psC = es.enter_context(nc.psum_tensor("psC", [P, 512], F32))
        psD = es.enter_context(nc.psum_tensor("psD", [P, 512], F32))
        psE = es.enter_context(nc.psum_tensor("psE", [P, 1024], F32))
        psF = es.enter_context(nc.psum_tensor("psF", [P, 1024], F32))
        bank_guard = {}

        def acquire(bname):
            g = bank_guard.get(bname)
            if g is not None:
                twait(g[0], g[1])

        def release(bname, sname):
            bank_guard[bname] = (sname, cnt.val[sname])

        for nm in ["wq", "wk", "wv", "wo"]:
            cnt.inc(nc.gpsimd.dma_start(
                ws[nm][:, :].rearrange("p (a n) -> p a n", a=KT),
                w_hbm[nm].rearrange("(a p) n -> p a n", p=P)), "ld", 16)
        cnt.inc(nc.gpsimd.dma_start(
            bq_pt[:, :], b_hbm["bq"].rearrange("(a p) -> p a", p=P),
            allow_slow_non_contiguous=True), "ld", 16)
        for nm in ["bk", "bv", "bo"]:
            cnt.inc(nc.gpsimd.dma_start(brow[nm], b_hbm[nm][None, :]), "ld", 16)

        with ExitStack() as es2:
            def sb2(name, shape, dtype, **kw):
                return es2.enter_context(nc.sbuf_tensor(name, shape, dtype, **kw))
            p_t = sb2("p_t", [1, 1], F32)
            iota_bc = sb2("iota_bc_s", [P, C], F32)
            iota_rev = sb2("iota_rev_s", [P, MT], F32)
            iota_cc = sb2("iota_cc_s", [P, NC2], F32)
            rflag = sb2("rflag_t", [P, 1], F32)
            expo = sb2("expo_s", [P, MT * C], F32)
            binm = sb2("binm_s", [P, MT * C], F32)
            ident = sb2("ident_s", [P, P], F32)
            bfrow_f = sb2("bfrow_f", [1, D], F32)
            wf_st = sb2("wf_st", [P, D], F32)
            wfq_b = sb2("wfq_b", [P, KT * D], BF16)
            wsq = sb2("wsq", [P, KT], F32)
            wrec = sb2("wrec", [P, KT], F32)
            wqi = sb2("wqi", [P, D], I32)
            g0s = sb2("g0s", [P, D], F32)
            g1s = sb2("g1s", [P, MT * C], F32)

            for ap_, h in [(p_t, p_hbm), (iota_bc, iota_bc_h), (iota_rev, iota_rev_h),
                           (expo, expo_h), (binm, binm_h), (ident, ident_h),
                           (bfrow_f, b_hbm["bf"][None, :]), (iota_cc, iota_cc_h),
                           (rflag, rflag_h), (sflag, sflag_h)]:
                cnt.inc(nc.sync.dma_start(ap_[:, :], h), "ld", 16)
            ld_consts = cnt.val["ld"]

            cnt.inc(nc.vector.memset(ones_row[:, :], 1.0), "v")
            cnt.inc(nc.vector.memset(ones_f32[:, :], 1.0), "v")
            cnt.inc(nc.vector.memset(eps_t[:, :], LN_EPS), "v")

            twait("ld", ld_consts)
            cnt.inc(nc.tensor.matmul(psA[:, :1], ones_f32[:1, :], p_t[:1, :1],
                                     start=True, stop=True), "pe")
            swait("pe")
            cnt.inc(nc.scalar.activation(logl[:, 1:2], psA[:, :1], AF.Sigmoid), "s")
            cnt.inc(nc.scalar.activation(logl[:, 0:1], logl[:, 1:2], AF.Ln), "s")
            vwait("s")
            cnt.inc(nc.vector.tensor_scalar(
                logl[:, 0:1], logl[:, 0:1], -1e30, None, A.max), "v")
            swait("v")
            cnt.inc(nc.scalar.activation(lp_bc[:, :], iota_bc[:, :], AF.Exp,
                                         scale=logl[:, :1]), "s")
            cnt.inc(nc.scalar.activation(lrev[:, :], iota_rev[:, :], AF.Exp,
                                         scale=logl[:, :1]), "s")
            cnt.inc(nc.scalar.activation(lamC[:, :], logl[:, :1], AF.Exp,
                                         scale=float(C)), "s")
            cnt.inc(nc.scalar.activation(lamcc[:, :], iota_cc[:, :], AF.Exp,
                                         scale=logl[:, :1]), "s")
            sflush()
            vwait("s")
            cnt.inc(nc.vector.tensor_scalar(
                lamcc[:, :], lamcc[:, :], rflag[:, :1], None, A.mult), "v")
            cnt.inc(nc.vector.tensor_scalar(g1s[:, :], expo[:, :], logl[:, :1],
                                            0.0, A.mult, A.min), "v")
            swait("v")
            cnt.inc(nc.scalar.activation(g1s[:, :], g1s[:, :], AF.Exp), "s")
            vwait("s")
            cnt.inc(nc.vector.tensor_tensor(
                maskt[:, :], g1s[:, :], binm[:, :], A.mult), "v")

            for a in range(KT):
                ywait("v")
                cnt.inc(nc.sync.dma_start(
                    wf_st[:, :], wf_hbm[a * P:(a + 1) * P, :]), "ld", 16)
                vwait("ld")
                cnt.inc(nc.vector.tensor_reduce(
                    wsq[:, a:a + 1], wf_st[:, :], AX.X, A.max,
                    apply_absolute_value=True), "v")
                vflush()
                cnt.inc(nc.vector.tensor_scalar(
                    wsq[:, a:a + 1], wsq[:, a:a + 1], 1e-8, QMAX_INV,
                    A.max, A.mult), "v")
                vflush()
                cnt.inc(nc.vector.reciprocal(wrec[:, a:a + 1], wsq[:, a:a + 1]), "v")
                vflush()
                cnt.inc(nc.vector.tensor_scalar(
                    wqi[:, :], wf_st[:, :], wrec[:, a:a + 1], None, A.mult), "v")
                cnt.inc(nc.vector.tensor_copy(
                    wfq_b[:, a * D:(a + 1) * D], wqi[:, :]), "v")
            v_wfq = cnt.val["v"]

            ywait("v", v_wfq)
            for a in range(KT):
                out_ap = bass.AP(wfT, a * P, [[KT * D, P], [D, KT], [1, P]])
                cnt.inc(nc.sync.dma_start_transpose(
                    out_ap, wfq_b[:, a * D:(a + 1) * D]), "trw", 16)

            twait("v", v_wfq)
            for a in range(KT):
                twait("v")
                cnt.inc(nc.tensor.transpose(
                    psA[:1, :P], wsq[:, a:a + 1], ident[:, :]), "pe")
                vwait("pe")
                cnt.inc(nc.vector.tensor_copy(
                    g0s[:1, a * P:(a + 1) * P], psA[:1, :P]), "v")
            for h in range(2):
                twait("v")
                cnt.inc(nc.tensor.matmul(psA[:, :], ones_f32[:1, :],
                                         g0s[:1, h * 512:(h + 1) * 512],
                                         start=True, stop=True), "pe")
                vwait("pe")
                cnt.inc(nc.vector.tensor_copy(
                    wsc_bc[:, h * 512:(h + 1) * 512], psA[:, :]), "v")
            for h in range(2):
                twait("v")
                cnt.inc(nc.tensor.matmul(psA[:, :], ones_f32[:1, :],
                                         bfrow_f[:1, h * 512:(h + 1) * 512],
                                         start=True, stop=True), "pe")
                vwait("pe")
                cnt.inc(nc.vector.tensor_copy(
                    bf_bc[:, h * 512:(h + 1) * 512], psA[:, :]), "v")
            cnt.inc(nc.vector.tensor_reduce(
                sbf_t[:, :], bf_bc[:, :], AX.X, A.add), "v")
            vflush()

        setup = {k: cnt.val[k] for k in ["v", "s", "trw", "ld", "pe"]}

        def barrier(vals):
            for sn, val in vals.items():
                vwait(sn, val)
                swait(sn, val)
                ywait(sn, val)
                twait(sn, val)
                gwait(sn, val)

        def ln_tail(src0, src1, junk, ctr, add_sbf=False):
            vflush()
            if add_sbf:
                cnt.inc(nc.vector.tensor_scalar(
                    STt[:, S_NEGM:S_NEGM + 2], STt[:, S_SUM:S_SUM + 2],
                    sbf_t[:, :1], -INV_D, A.add, A.mult), "v")
            else:
                cnt.inc(nc.vector.tensor_scalar(
                    STt[:, S_NEGM:S_NEGM + 2], STt[:, S_SUM:S_SUM + 2],
                    -INV_D, None, A.mult), "v")
            swait("v")
            cnt.inc(nc.scalar.activation(
                junk[:, 0:D], src0,
                AF.Square, bias=STt[:, S_NEGM:S_NEGM + 1],
                accum_out=STt[:, S_SSQ:S_SSQ + 1]), "s")
            cnt.inc(nc.scalar.activation(
                junk[:, 0:D], src1,
                AF.Square, bias=STt[:, S_NEGM + 1:S_NEGM + 2],
                accum_out=STt[:, S_SSQ + 1:S_SSQ + 2]), "s")
            sflush()
            cnt.inc(nc.scalar.activation(
                STt[:, S_LNT:S_LNT + 2], STt[:, S_SSQ:S_SSQ + 2], AF.Ln,
                scale=INV_D, bias=eps_t[:, :1]), "s")
            cnt.inc(nc.scalar.activation(
                STt[:, S_INV:S_INV + 2], STt[:, S_LNT:S_LNT + 2], AF.Exp,
                scale=-0.5), "s")
            vflush()
            cnt.inc(nc.vector.tensor_scalar(
                ctr[:, 0:D], src0, STt[:, S_NEGM:S_NEGM + 1], None, A.add), "v")
            cnt.inc(nc.vector.tensor_scalar(
                ctr[:, D:2 * D], src1, STt[:, S_NEGM + 1:S_NEGM + 2], None,
                A.add), "v")

        def ln_center2(src0, src1, junk, ctr):
            cnt.inc(nc.vector.tensor_reduce(
                STt[:, S_SUM:S_SUM + 1], src0, AX.X, A.add), "v")
            cnt.inc(nc.vector.tensor_reduce(
                STt[:, S_SUM + 1:S_SUM + 2], src1, AX.X, A.add), "v")
            ln_tail(src0, src1, junk, ctr)

        with ExitStack() as esA:
            def sbA(name, shape, dtype, **kw):
                return esA.enter_context(nc.sbuf_tensor(name, shape, dtype, **kw))
            S_f = sbA("S_f", [P, KT * D], F32)
            S_b = sbA("S_b", [P, KT * D], BF16)
            xbA = [sbA(f"xbA{i}", [P, MT * D], F32) for i in range(2)]
            xn1b = sbA("xn1b", [P, MT * D], BF16)
            xn1T = sbA("xn1T", [P, KT * C], BF16, is_transpose_dest=True)
            qT = sbA("qT", [P, KT * C], BF16)
            qTs = sbA("qTs", [P, KT * C], BF16)
            k_tm = sbA("k_tm", [P, MT * D], BF16)
            k_lm = sbA("k_lm", [P, MT * D], BF16)
            kTb = sbA("kTb", [P, KT * C], BF16, is_transpose_dest=True)
            v_tm = sbA("v_tm", [P, MT * D], BF16)
            at_b = sbA("at_b", [P, MT * C], BF16)
            G0 = sbA("G0a", [P, MT * D], F32)
            G1 = sbA("G1a", [P, MT * D], F32)

            barrier(setup)
            cnt.inc(nc.vector.memset(S_f[:, :], 0.0), "v")
            cnt.inc(nc.vector.memset(S_b[:, :], 0.0), "v")
            vz = cnt.val["v"]
            cnt.wait(nc.scalar, "s", "v", vz)
            cnt.inc(nc.scalar.dma_start(sp_S[0:P, :], S_b[:, :]), "sps", 16)

            def load_xA(c):
                sm = "ldx0" if c % 2 == 0 else "ldx1"
                cnt.inc(nc.gpsimd.dma_start(
                    xbA[c % 2][:, :].rearrange("p (a d) -> p a d", a=MT),
                    xin[c * C:(c + 1) * C, :].rearrange("(a p) d -> p a d", p=P)),
                    sm, 16)
                LT[("x", c)] = (sm, cnt.val[sm])

            load_xA(0)
            pe_vproj = [0] * (NC2 + 1)

            def emit_front(c):
                sm, tv = LT[("x", c)]
                xc = xbA[c % 2]
                vwait(sm, tv)
                swait(sm, tv)
                ln_center2(xc[:, 0:D], xc[:, D:2 * D], G0, G1)
                vwait("s")
                for mt in range(MT):
                    cnt.inc(nc.vector.tensor_scalar(
                        xn1b[:, mt * D:(mt + 1) * D], G1[:, mt * D:(mt + 1) * D],
                        STt[:, S_INV + mt:S_INV + mt + 1], None, A.mult), "v")
                v_xn1 = cnt.val["v"]
                ywait("v", v_xn1)
                if c > 0:
                    ywait("pe", pe_vproj[c - 1])
                for mt in range(MT):
                    out_ap = bass.AP(xn1T, mt * P, [[KT * C, P], [C, KT], [1, P]])
                    cnt.inc(nc.sync.dma_start_transpose(
                        out_ap, xn1b[:, mt * D:(mt + 1) * D]), "trx", 16)
                return cnt.val["trx"], cnt.val["s"]

            tr_front, s_front = emit_front(0)
            s_sb_prev = 0
            stt_v = [0] * KT

            for c in range(NC2):
                if c + 1 < NC2:
                    load_xA(c + 1)
                tr_xn1 = tr_front

                twait("trx", tr_xn1)
                kb = [(psE, "E"), (psF, "F")]
                pe_k = [0, 0]
                for mt in range(MT):
                    bank, bname = kb[mt]
                    acquire(bname)
                    for h in range(2):
                        for kk in range(KT):
                            mm = nc.tensor.matmul(
                                bank[:, h * 512:(h + 1) * 512],
                                xn1T[:, kk * C + mt * P: kk * C + (mt + 1) * P],
                                ws["wk"][:, kk * D + h * 512: kk * D + (h + 1) * 512],
                                start=(kk == 0), stop=False)
                        mm = nc.tensor.matmul(
                            bank[:, h * 512:(h + 1) * 512], ones_row[0:1, :P],
                            brows[0:1, h * 512:(h + 1) * 512],
                            start=False, stop=True)
                    cnt.inc(mm, "pe")
                    pe_k[mt] = cnt.val["pe"]
                for mt in range(MT):
                    vwait("pe", pe_k[mt])
                    vwait("s", s_front)
                    cnt.inc(nc.vector.tensor_scalar(
                        G0[:, mt * D:(mt + 1) * D], kb[mt][0][:, :],
                        0.0, None, A.min), "v")
                swait("v")
                for mt in range(MT):
                    cnt.inc(nc.scalar.activation(
                        G1[:, mt * D:(mt + 1) * D], G0[:, mt * D:(mt + 1) * D],
                        AF.Exp), "s")
                vwait("s")
                vwait("trk")
                for mt in range(MT):
                    cnt.inc(nc.vector.scalar_tensor_tensor(
                        k_tm[:, mt * D:(mt + 1) * D], kb[mt][0][:, :], 0.0,
                        G1[:, mt * D:(mt + 1) * D], A.max, A.add), "v")
                    release(kb[mt][1], "v")
                v_ktm = cnt.val["v"]

                ywait("v", v_ktm)
                for mt in range(MT):
                    out_ap = bass.AP(kTb, mt * P, [[KT * C, P], [C, KT], [1, P]])
                    cnt.inc(nc.sync.dma_start_transpose(
                        out_ap, k_tm[:, mt * D:(mt + 1) * D]), "trk", 16)
                tr_k = cnt.val["trk"]
                for mt in range(MT):
                    cnt.inc(nc.vector.tensor_scalar(
                        k_lm[:, mt * D:(mt + 1) * D], k_tm[:, mt * D:(mt + 1) * D],
                        lrev[:, mt:mt + 1], None, A.mult), "v")
                v_klam = cnt.val["v"]

                for i, (bank, bname) in enumerate(
                        [(psA, "A"), (psB, "B"), (psC, "C"), (psD, "D")]):
                    mt, h = divmod(i, 2)
                    acquire(bname)
                    for kk in range(KT):
                        mm = nc.tensor.matmul(
                            bank[:, :],
                            xn1T[:, kk * C + mt * P: kk * C + (mt + 1) * P],
                            ws["wv"][:, kk * D + h * 512: kk * D + (h + 1) * 512],
                            start=(kk == 0), stop=False)
                    mm = nc.tensor.matmul(
                        bank[:, :], ones_row[32:33, :P],
                        brows[32:33, h * 512:(h + 1) * 512], start=False, stop=True)
                    cnt.inc(mm, "pe")
                    swait("pe")
                    if i == 0:
                        swait("spv")
                    cnt.inc(nc.scalar.activation(
                        v_tm[:, mt * D + h * 512: mt * D + (h + 1) * 512],
                        bank[:, :], AF.Copy), "s")
                    release(bname, "s")
                pe_vproj[c] = cnt.val["pe"]
                s_vtm = cnt.val["s"]
                gwait("s", s_vtm)
                cnt.inc(nc.gpsimd.dma_start(
                    sp_vt[c * P:(c + 1) * P, :], v_tm[:, :]), "spv", 16)

                for g in range(4):
                    bank, bname = (psA, "A") if g % 2 == 0 else (psB, "B")
                    acquire(bname)
                    for mi in range(2):
                        m = 2 * g + mi
                        for kk in range(KT):
                            mm = nc.tensor.matmul(
                                bank[:, mi * C:(mi + 1) * C],
                                ws["wq"][:, kk * D + m * P: kk * D + (m + 1) * P],
                                xn1T[:, kk * C:(kk + 1) * C],
                                start=(kk == 0), stop=(kk == KT - 1))
                    cnt.inc(mm, "pe")
                    swait("pe")
                    for mi in range(2):
                        m = 2 * g + mi
                        cnt.inc(nc.scalar.activation(
                            qT[:, m * C:(m + 1) * C], bank[:, mi * C:(mi + 1) * C],
                            AF.Identity, bias=bq_pt[:, m:m + 1]), "s")
                    release(bname, "s")
                s_qev = cnt.val["s"]
                vwait("s", s_qev)
                vwait("spq")
                for kk in range(KT):
                    cnt.inc(nc.vector.tensor_tensor(
                        qTs[:, kk * C:(kk + 1) * C], qT[:, kk * C:(kk + 1) * C],
                        lp_bc[:, :], A.mult), "v")
                v_qts = cnt.val["v"]
                gwait("v", v_qts)
                cnt.inc(nc.gpsimd.dma_start(
                    sp_qTs[c * P:(c + 1) * P, :], qTs[:, :]), "spq", 16)

                if c + 1 < NC2:
                    tr_front, s_front = emit_front(c + 1)

                acquire("A")
                twait("trk", tr_k)
                twait("s", s_qev)
                for mtk in range(MT):
                    for kk in range(KT):
                        mm = nc.tensor.matmul(
                            psA[:, mtk * 256:(mtk + 1) * 256],
                            kTb[:, kk * C + mtk * P: kk * C + (mtk + 1) * P],
                            qT[:, kk * C:(kk + 1) * C],
                            start=(kk == 0), stop=(kk == KT - 1))
                cnt.inc(mm, "pe")
                vwait("pe")
                vwait("spa")
                cnt.inc(nc.vector.tensor_tensor(
                    at_b[:, :], psA[:, :], maskt[:, :], A.mult), "v")
                release("A", "v")
                v_atb = cnt.val["v"]
                gwait("v", v_atb)
                cnt.inc(nc.gpsimd.dma_start(
                    sp_at[c * P:(c + 1) * P, :], at_b[:, :]), "spa", 16)

                twait("v", v_klam)
                vwait("s", s_sb_prev)
                for i in range(KT):
                    bank, bname = (psF, "F") if i % 2 == 0 else (psE, "E")
                    acquire(bname)
                    for h in range(2):
                        for a in range(MT):
                            mm = nc.tensor.matmul(
                                bank[:, h * 512:(h + 1) * 512],
                                k_lm[:, a * D + i * P: a * D + (i + 1) * P],
                                v_tm[:, a * D + h * 512: a * D + (h + 1) * 512],
                                start=(a == 0), stop=(a == MT - 1))
                    cnt.inc(mm, "pe")
                    vwait("pe")
                    cnt.inc(nc.vector.scalar_tensor_tensor(
                        S_f[:, i * D:(i + 1) * D], S_f[:, i * D:(i + 1) * D],
                        lamC[:, :1], bank[:, :], A.mult, A.add), "v")
                    release(bname, "v")
                    stt_v[i] = cnt.val["v"]
                v_supd = cnt.val["v"]
                swait("sps")
                for i in range(KT):
                    swait("v", stt_v[i])
                    cnt.inc(nc.scalar.activation(
                        S_b[:, i * D:(i + 1) * D], S_f[:, i * D:(i + 1) * D],
                        AF.Copy), "s")
                s_sb_prev = cnt.val["s"]
                gwait("s", s_sb_prev)
                cnt.inc(nc.gpsimd.dma_start(
                    sp_S[(c + 1) * P:(c + 2) * P, :], S_b[:, :]), "sps", 16)

            vwait("sps")
            cnt.inc(nc.vector.tensor_scalar(
                S_b[:, :], S_b[:, :], sflag[:, :1], None, A.mult), "v")
            gwait("v")
            cnt.inc(nc.gpsimd.dma_start(cc_in.ap(), S_b[:, :]), "spc", 16)
            gwait("spc")
            cnt.inc(nc.gpsimd.collective_compute(
                "AllReduce", A.add,
                replica_groups=[[2 * i, 2 * i + 1] for i in range(nb)],
                ins=[cc_in.ap()], outs=[cc_out.ap()]), "cc", 1)

            phA = {k: cnt.val[k] for k in ["v", "s", "trx", "trk", "pe",
                                           "ldx0", "ldx1",
                                           "spq", "spa", "spv", "sps", "spc"]}

        with ExitStack() as esB:
            def sbB(name, shape, dtype, **kw):
                return esB.enter_context(nc.sbuf_tensor(name, shape, dtype, **kw))
            M_r = sbB("M_r", [P, KT * D], BF16)
            Sc = [sbB(f"Sc{i}", [P, KT * D], BF16) for i in range(2)]
            xbB = [sbB(f"xbB{i}", [P, MT * D], F32) for i in range(2)]
            qTs_l = [sbB(f"qTs_l{i}", [P, KT * C], BF16) for i in range(2)]
            qTg = sbB("qTg", [P, KT * C], BF16)
            at_l = sbB("at_l", [P, MT * C], BF16)
            vt_l = sbB("vt_l", [P, MT * D], BF16)
            rdT = sbB("rdT", [P, KT * C], BF16)
            G0 = sbB("G0b", [P, MT * D], F32)
            G1 = sbB("G1b", [P, MT * D], F32)
            I1 = sbB("I1", [P, 512], I32)
            xqb = sbB("xqb", [P, MT * D], BF16)
            xqT = sbB("xqT", [P, MT * D], BF16, is_transpose_dest=True)

            barrier(phA)
            gwait("cc")
            cnt.inc(nc.gpsimd.dma_start(M_r[:, :], cc_out.ap()), "ldm2", 16)
            twait("ldm2", 16)

            ldv = {}

            def load_xB(c):
                sm = "ldx0" if c % 2 == 0 else "ldx1"
                cnt.inc(nc.gpsimd.dma_start(
                    xbB[c % 2][:, :].rearrange("p (a d) -> p a d", a=MT),
                    xin[c * C:(c + 1) * C, :].rearrange("(a p) d -> p a d", p=P)),
                    sm, 16)
                LT[("bx", c)] = (sm, cnt.val[sm])

            def load_Sc(c):
                sm = "lds0" if c % 2 == 0 else "lds1"
                cnt.inc(nc.gpsimd.dma_start(
                    Sc[c % 2][:, :], sp_S[c * P:(c + 1) * P, :]), sm, 16)
                LT[("sc", c)] = (sm, cnt.val[sm])

            def load_small(c):
                cnt.inc(nc.gpsimd.dma_start(
                    qTs_l[c % 2][:, :], sp_qTs[c * P:(c + 1) * P, :]), "ldsm", 16)
                ldv[c] = cnt.val["ldsm"]

            def load_at(c):
                cnt.inc(nc.gpsimd.dma_start(
                    at_l[:, :], sp_at[c * P:(c + 1) * P, :]), "ldat", 16)
                cnt.inc(nc.gpsimd.dma_start(
                    vt_l[:, :], sp_vt[c * P:(c + 1) * P, :]), "ldat", 16)
                LT[("at", c)] = ("ldat", cnt.val["ldat"])

            load_xB(0)
            load_Sc(0)
            load_small(0)
            load_at(0)
            v_g0p = 0
            pe_rd = [0] * (NC2 + 1)
            pe_grp = [[0] * 4 for _ in range(NC2 + 1)]
            s_rd = [0] * (NC2 + 1)
            st_done = [0, 0]
            if nch_b > 1:
                load_xB(1)
                load_Sc(1)
                load_small(1)

            def emit_readT(c):
                vwait("ldsm", ldv[c])
                cnt.inc(nc.vector.tensor_scalar(
                    qTg[:, :], qTs_l[c % 2][:, :], lamcc[:, c:c + 1], None,
                    A.mult), "v")
                v_qtg = cnt.val["v"]
                twait("ldsm", ldv[c])
                twait(*LT[("sc", c)])
                twait(*LT[("at", c)])
                twait("v", v_qtg)
                for gi, (bank, bname) in enumerate(
                        [(psA, "A"), (psB, "B"), (psC, "C"), (psD, "D")]):
                    acquire(bname)
                    for half in range(2):
                        j = gi * 2 + half
                        tgt = bank[:, half * C:(half + 1) * C]
                        for a in range(MT):
                            mm = nc.tensor.matmul(
                                tgt,
                                vt_l[:, a * D + j * P: a * D + (j + 1) * P],
                                at_l[:, a * C:(a + 1) * C],
                                start=(a == 0), stop=False)
                        for kk in range(KT):
                            mm = nc.tensor.matmul(
                                tgt, Sc[c % 2][:, kk * D + j * P: kk * D + (j + 1) * P],
                                qTs_l[c % 2][:, kk * C:(kk + 1) * C],
                                start=False, stop=False)
                        for kk in range(KT):
                            mm = nc.tensor.matmul(
                                tgt, M_r[:, kk * D + j * P: kk * D + (j + 1) * P],
                                qTg[:, kk * C:(kk + 1) * C],
                                start=False, stop=(kk == KT - 1))
                    cnt.inc(mm, "pe")
                    pe_grp[c][gi] = cnt.val["pe"]
                pe_rd[c] = cnt.val["pe"]

            def emit_rd_evacs(c):
                for gi, (bank, bname) in enumerate(
                        [(psA, "A"), (psB, "B"), (psC, "C"), (psD, "D")]):
                    swait("pe", pe_grp[c][gi])
                    for half in range(2):
                        j = gi * 2 + half
                        cnt.inc(nc.scalar.activation(
                            rdT[:, j * C:(j + 1) * C],
                            bank[:, half * C:(half + 1) * C], AF.Copy), "s")
                    release(bname, "s")
                s_rd[c] = cnt.val["s"]

            emit_readT(0)
            emit_rd_evacs(0)

            for c in range(nch_b):
                if c + 2 < nch_b:
                    cnt.wait(nc.gpsimd, "g", "pe", pe_rd[c])
                    load_small(c + 2)
                if c + 1 < nch_b:
                    cnt.wait(nc.gpsimd, "g", "pe", pe_rd[c])
                    load_at(c + 1)
                xc = xbB[c % 2]

                twait("s", s_rd[c])
                wb = [(psF, "F"), (psE, "E")]
                pe_wo = [0, 0]
                for mt in range(MT):
                    bank, bname = wb[mt]
                    acquire(bname)
                    for h in range(2):
                        for kk in range(KT):
                            mm = nc.tensor.matmul(
                                bank[:, h * 512:(h + 1) * 512],
                                rdT[:, kk * C + mt * P: kk * C + (mt + 1) * P],
                                ws["wo"][:, kk * D + h * 512: kk * D + (h + 1) * 512],
                                start=(kk == 0), stop=False)
                        mm = nc.tensor.matmul(
                            bank[:, h * 512:(h + 1) * 512], ones_row[64:65, :P],
                            brows[64:65, h * 512:(h + 1) * 512],
                            start=False, stop=True)
                    cnt.inc(mm, "pe")
                    pe_wo[mt] = cnt.val["pe"]
                for mt in range(MT):
                    swait("pe", pe_wo[mt])
                    swait("v", v_g0p)
                    cnt.inc(nc.scalar.activation(
                        G0[:, mt * D:(mt + 1) * D], wb[mt][0][:, :], AF.Identity,
                        accum_out=STt[:, S_SUM + mt:S_SUM + mt + 1]), "s")
                    release(wb[mt][1], "s")
                vwait("s")
                ln_tail(G0[:, 0:D], G0[:, D:2 * D], xqb, G1)
                vwait("s")
                vwait(*LT[("bx", c)])
                for mt in range(MT):
                    cnt.inc(nc.vector.scalar_tensor_tensor(
                        xc[:, mt * D:(mt + 1) * D], G1[:, mt * D:(mt + 1) * D],
                        STt[:, S_INV + mt:S_INV + mt + 1],
                        xc[:, mt * D:(mt + 1) * D], A.mult, A.add,
                        accum_out=STt[:, S_SUM + mt:S_SUM + mt + 1]), "v")

                if c + 1 < nch_b:
                    emit_readT(c + 1)

                ln_tail(xc[:, 0:D], xc[:, D:2 * D], xqb, G1)
                vwait("s")
                for mt in range(MT):
                    cnt.inc(nc.vector.tensor_scalar(
                        G0[:, mt * D:(mt + 1) * D], G1[:, mt * D:(mt + 1) * D],
                        STt[:, S_INV + mt:S_INV + mt + 1], None, A.mult), "v")
                for mt in range(MT):
                    cnt.inc(nc.vector.tensor_reduce(
                        STt[:, S_XS + mt:S_XS + mt + 1],
                        G0[:, mt * D:(mt + 1) * D],
                        AX.X, A.max, apply_absolute_value=True), "v")
                vflush()
                cnt.inc(nc.vector.tensor_scalar(
                    STt[:, S_XS:S_XS + 2], STt[:, S_XS:S_XS + 2], 1e-8, QMAX_INV,
                    A.max, A.mult), "v")
                vflush()
                cnt.inc(nc.vector.reciprocal(
                    STt[:, S_REC:S_REC + 2], STt[:, S_XS:S_XS + 2]), "v")
                vflush()
                tr_base = cnt.val["trq"]
                vwait("trq", tr_base)
                pe_ff = [0, 0]
                for mt in range(MT):
                    for h in range(2):
                        cnt.inc(nc.vector.tensor_scalar(
                            I1[:, :], G0[:, mt * D + h * 512: mt * D + (h + 1) * 512],
                            STt[:, S_REC + mt:S_REC + mt + 1], None, A.mult), "v")
                        cnt.inc(nc.vector.tensor_copy(
                            xqb[:, mt * D + h * 512: mt * D + (h + 1) * 512],
                            I1[:, :]), "v")
                    v_xq = cnt.val["v"]
                    ywait("v", v_xq)
                    out_ap = bass.AP(xqT, mt * D, [[MT * D, P], [P, KT], [1, P]])
                    cnt.inc(nc.sync.dma_start_transpose(
                        out_ap, xqb[:, mt * D:(mt + 1) * D]), "trq", 16)
                for mt in range(MT):
                    bank, bname = wb[mt]
                    twait("trq", tr_base + 16 * (mt + 1))
                    acquire(bname)
                    for h in range(2):
                        for kk in range(KT):
                            mm = nc.tensor.matmul(
                                bank[:, h * 512:(h + 1) * 512],
                                xqT[:, mt * D + kk * P: mt * D + (kk + 1) * P],
                                wfT[:, kk * D + h * 512: kk * D + (h + 1) * 512],
                                start=(kk == 0), stop=(kk == KT - 1))
                    cnt.inc(mm, "pe")
                    pe_ff[mt] = cnt.val["pe"]
                for mt in range(MT):
                    vwait("pe", pe_ff[mt])
                    cnt.inc(nc.vector.scalar_tensor_tensor(
                        G1[:, mt * D:(mt + 1) * D], wb[mt][0][:, :],
                        STt[:, S_XS + mt:S_XS + mt + 1], wsc_bc[:, :],
                        A.mult, A.mult,
                        accum_out=STt[:, S_SUM + mt:S_SUM + mt + 1]), "v")
                    release(wb[mt][1], "v")
                    cnt.inc(nc.vector.tensor_tensor(
                        G0[:, mt * D:(mt + 1) * D], G1[:, mt * D:(mt + 1) * D],
                        bf_bc[:, :], A.add), "v")
                ln_tail(G0[:, 0:D], G0[:, D:2 * D], xqb, G1, add_sbf=True)
                v_g0p = cnt.val["v"]
                vwait("s")
                for mt in range(MT):
                    cnt.inc(nc.vector.scalar_tensor_tensor(
                        xc[:, mt * D:(mt + 1) * D], G1[:, mt * D:(mt + 1) * D],
                        STt[:, S_INV + mt:S_INV + mt + 1],
                        xc[:, mt * D:(mt + 1) * D], A.mult, A.add), "v")
                v_out = cnt.val["v"]
                gwait("v", v_out)
                for mt in range(MT):
                    cnt.inc(nc.gpsimd.dma_start(
                        yout[c * C + mt * P: c * C + (mt + 1) * P, :],
                        xc[:, mt * D:(mt + 1) * D]),
                        "st0" if c % 2 == 0 else "st1", 16)
                st_done[c % 2] = cnt.val["st0" if c % 2 == 0 else "st1"]

                if c + 1 < nch_b:
                    emit_rd_evacs(c + 1)

                if c + 2 < nch_b:
                    cnt.wait(nc.gpsimd, "g", "st" + str(c % 2), st_done[c % 2])
                    load_xB(c + 2)
                    load_Sc(c + 2)

            if taps:
                local = {"qTs_l": qTs_l[(nch_b - 1) % 2],
                         "at_l": at_l,
                         "vt_l": vt_l,
                         "Sc0": Sc[(nch_b - 1) % 2], "rdT": rdT, "M_r": M_r,
                         "qTg": qTg, "xb0": xbB[0], "xb1": xbB[1]}
                ywait("v")
                ywait("s")
                for nm, ap_h in dbg.items():
                    if nm not in local:
                        continue
                    src = local[nm]
                    wtot = ap_h.shape[1]
                    for off in range(0, wtot, D):
                        wd = min(D, wtot - off)
                        ywait("v")
                        cnt.inc(nc.vector.tensor_copy(G0[:, :wd],
                                                      src[:, off:off + wd]), "v")
                        ywait("v")
                        cnt.inc(nc.sync.dma_start(ap_h[:, off:off + wd],
                                                  G0[:, :wd]), "st", 16)
                        ywait("st")
                        cnt.wait(nc.vector, "v", "st", cnt.val["st"])
            nc.sync.wait_ge(cnt.st0, cnt.val["st0"])
            if cnt.val["st1"]:
                nc.sync.wait_ge(cnt.st1, cnt.val["st1"])
            if cnt.val["st"]:
                nc.sync.wait_ge(cnt.st, cnt.val["st"])
    return nc


def kernel(L0_attn_x, L0_attn_Wq, L0_attn_bq, L0_attn_Wk, L0_attn_bk,
           L0_attn_Wv, L0_attn_bv, L0_attn_Wo, L0_attn_bo,
           L0_attn_decay_param, L0_ffn_weight, L0_ffn_bias):
    x = np.ascontiguousarray(np.asarray(L0_attn_x, dtype=np.float32))
    B = x.shape[0]
    iota_bc, iota_rev, expo, binm, ident = _consts()
    iota_cc = np.broadcast_to(
        (np.arange(NC2, dtype=np.float32) * C)[None, :], (P, NC2)).copy()
    common = {
        "wq": np.ascontiguousarray(L0_attn_Wq, dtype=np.float32),
        "wk": np.ascontiguousarray(L0_attn_Wk, dtype=np.float32),
        "wv": np.ascontiguousarray(L0_attn_Wv, dtype=np.float32),
        "wo": np.ascontiguousarray(L0_attn_Wo, dtype=np.float32),
        "wf": np.ascontiguousarray(L0_ffn_weight, dtype=np.float32),
        "bq": np.ascontiguousarray(L0_attn_bq, dtype=np.float32),
        "bk": np.ascontiguousarray(L0_attn_bk, dtype=np.float32),
        "bv": np.ascontiguousarray(L0_attn_bv, dtype=np.float32),
        "bo": np.ascontiguousarray(L0_attn_bo, dtype=np.float32),
        "bf": np.ascontiguousarray(L0_ffn_bias, dtype=np.float32),
        "decay": np.asarray(L0_attn_decay_param, dtype=np.float32).reshape(1, 1),
        "iota_bc": iota_bc, "iota_rev": iota_rev,
        "expo": expo, "binm": binm, "ident": ident, "iota_cc": iota_cc,
    }
    nc = build_nc2(nb=B)
    in_maps = []
    for core in range(2 * B):
        b, h = core // 2, core % 2
        m = dict(common,
                 xin=np.ascontiguousarray(x[b, h * TH:(h + 1) * TH]),
                 rflag=np.full((P, 1), float(h), np.float32),
                 sflag=np.full((P, 1), float(1 - h), np.float32))
        in_maps.append(m)
    res = run_bass_kernel_spmd(nc, in_maps, core_ids=list(range(2 * B)))
    out = np.empty((B, T, D), np.float32)
    for core in range(2 * B):
        b, h = core // 2, core % 2
        out[b, h * TH:(h + 1) * TH] = np.asarray(res.results[core]["yout"])
    return out


# version 21
# speedup vs baseline: 1.0522x; 1.0522x over previous
import numpy as np
from contextlib import ExitStack

import concourse.bass as bass
import concourse.mybir as mybir
from concourse.bass_utils import run_bass_kernel_spmd

F32 = mybir.dt.float32
BF16 = mybir.dt.bfloat16
I32 = mybir.dt.int32
A = mybir.AluOpType
AF = mybir.ActivationFunctionType
AX = mybir.AxisListType

P = 128
D = 1024
KT = D // P
C = 256
MT = C // P
T = 8192
NCH = T // C
INV_D = 1.0 / D
QMAX_INV = 1.0 / 127.0
LN_EPS = 1e-5
S_NEGM, S_INV, S_SSQ, S_XS, S_REC, S_SUM, S_LNT = 0, 2, 4, 6, 8, 10, 12


class Counts:

    def __init__(self, nc):
        self.nc = nc
        self.val = {}
        self.last_wait = {}

    def sem(self, name, es):
        s = es.enter_context(self.nc.semaphore(name))
        self.val[name] = 0
        setattr(self, name, s)

    def inc(self, instr, name, n=1):
        instr.then_inc(getattr(self, name), n)
        self.val[name] += n
        return self.val[name]

    def wait(self, eng, ekey, sname, value=None):
        v = self.val[sname] if value is None else value
        if v <= 0:
            return
        key = (ekey, sname)
        if self.last_wait.get(key, -1) >= v:
            return
        eng.wait_ge(getattr(self, sname), v)
        self.last_wait[key] = v


def build_nc(nch=NCH, taps=False):
    nc = bass.Bass("TRN2", target_bir_lowering=False)

    xin = nc.dram_tensor("xin", [T, D], F32, kind="ExternalInput").ap()
    w_hbm = {nm: nc.dram_tensor(nm, [D, D], F32, kind="ExternalInput").ap()
             for nm in ["wq", "wk", "wv", "wo"]}
    wf_hbm = nc.dram_tensor("wf", [D, D], F32, kind="ExternalInput").ap()
    b_hbm = {nm: nc.dram_tensor(nm, [D], F32, kind="ExternalInput").ap()
             for nm in ["bq", "bk", "bv", "bo", "bf"]}
    p_hbm = nc.dram_tensor("decay", [1, 1], F32, kind="ExternalInput").ap()
    iota_bc_h = nc.dram_tensor("iota_bc", [P, C], F32, kind="ExternalInput").ap()
    iota_rev_h = nc.dram_tensor("iota_rev", [P, MT], F32, kind="ExternalInput").ap()
    expo_h = nc.dram_tensor("expo", [P, MT * C], F32, kind="ExternalInput").ap()
    binm_h = nc.dram_tensor("binm", [P, MT * C], F32, kind="ExternalInput").ap()
    ident_h = nc.dram_tensor("ident", [P, P], F32, kind="ExternalInput").ap()
    yout = nc.dram_tensor("yout", [T, D], F32, kind="ExternalOutput").ap()
    tap_specs = [("xn1b", [P, MT * D]), ("qT", [P, KT * C]), ("qTs", [P, KT * C]),
                 ("k_tm", [P, MT * D]), ("kTb", [P, KT * C]), ("v_tm", [P, MT * D]),
                 ("at_b", [P, MT * C]), ("rdT", [P, KT * C]), ("S_f", [P, KT * D]),
                 ("xqb", [P, D]), ("maskt", [P, MT * C]), ("lp_bc", [P, C]),
                 ("STt", [P, 16])]
    dbg = {}
    if taps:
        for nm, shape in tap_specs:
            dbg[nm] = nc.dram_tensor("dbg_" + nm, shape, F32, kind="ExternalOutput").ap()

    with ExitStack() as es:
        def sb(name, shape, dtype, **kw):
            return es.enter_context(nc.sbuf_tensor(name, shape, dtype, **kw))

        cnt = Counts(nc)
        for s in ["ld", "tr", "st", "pe", "v", "s"]:
            cnt.sem(s, es)

        def twait(sn, v=None):
            cnt.wait(nc.tensor, "t", sn, v)

        def vwait(sn, v=None):
            cnt.wait(nc.vector, "v", sn, v)

        def swait(sn, v=None):
            cnt.wait(nc.scalar, "s", sn, v)

        def gwait(sn, v=None):
            cnt.wait(nc.gpsimd, "g", sn, v)

        def ywait(sn, v=None):
            cnt.wait(nc.sync, "y", sn, v)

        def vflush():
            cnt.wait(nc.vector, "v", "v")

        def sflush():
            cnt.wait(nc.scalar, "s", "s")

        ws = {nm: sb(nm + "_s", [P, KT * D], BF16) for nm in ["wq", "wk", "wv", "wo"]}
        wfT = sb("wfT_s", [P, KT * D], BF16)
        S_f = sb("S_f", [P, KT * D], F32)
        S_b = sb("S_b", [P, KT * D], BF16)
        bq_pt = sb("bq_pt", [P, KT], F32)
        brows = sb("brows", [P, D], BF16)
        brow = {"bk": brows[0:1, :], "bv": brows[32:33, :], "bo": brows[64:65, :]}
        ones_row = sb("ones_row", [P, C], BF16)
        ones_f32 = sb("ones_f32", [1, P], F32)
        wsc_bc = sb("wsc_bc", [P, D], F32)
        bf_bc = sb("bf_bc", [P, D], F32)
        maskt = sb("maskt", [P, MT * C], BF16)
        lp_bc = sb("lp_bc", [P, C], BF16)
        lrev = sb("lrev", [P, MT], F32)
        lamC = sb("lamC", [P, 1], F32)
        logl = sb("logl", [P, 2], F32)
        STt = sb("stats", [P, 16], F32)
        eps_t = sb("eps_t", [P, 1], F32)

        psA = es.enter_context(nc.psum_tensor("psA", [P, 512], F32))
        psB = es.enter_context(nc.psum_tensor("psB", [P, 512], F32))
        psC = es.enter_context(nc.psum_tensor("psC", [P, 512], F32))
        psD = es.enter_context(nc.psum_tensor("psD", [P, 512], F32))
        psE = es.enter_context(nc.psum_tensor("psE", [P, 1024], F32))
        psF = es.enter_context(nc.psum_tensor("psF", [P, 1024], F32))
        bank_guard = {}

        def acquire(bname):
            g = bank_guard.get(bname)
            if g is not None:
                twait(g[0], g[1])

        def release(bname, sname):
            bank_guard[bname] = (sname, cnt.val[sname])

        for nm in ["wq", "wk", "wv", "wo"]:
            cnt.inc(nc.gpsimd.dma_start(
                ws[nm][:, :].rearrange("p (a n) -> p a n", a=KT),
                w_hbm[nm].rearrange("(a p) n -> p a n", p=P)), "ld", 16)
        cnt.inc(nc.gpsimd.dma_start(
            bq_pt[:, :], b_hbm["bq"].rearrange("(a p) -> p a", p=P),
            allow_slow_non_contiguous=True), "ld", 16)
        for nm in ["bk", "bv", "bo"]:
            cnt.inc(nc.gpsimd.dma_start(brow[nm], b_hbm[nm][None, :]), "ld", 16)

        with ExitStack() as es2:
            def sb2(name, shape, dtype, **kw):
                return es2.enter_context(nc.sbuf_tensor(name, shape, dtype, **kw))
            p_t = sb2("p_t", [1, 1], F32)
            iota_bc = sb2("iota_bc_s", [P, C], F32)
            iota_rev = sb2("iota_rev_s", [P, MT], F32)
            expo = sb2("expo_s", [P, MT * C], F32)
            binm = sb2("binm_s", [P, MT * C], F32)
            ident = sb2("ident_s", [P, P], F32)
            bfrow_f = sb2("bfrow_f", [1, D], F32)
            wf_st = sb2("wf_st", [P, D], F32)
            wfq_b = sb2("wfq_b", [P, KT * D], BF16)
            wsq = sb2("wsq", [P, KT], F32)
            wrec = sb2("wrec", [P, KT], F32)
            wqi = sb2("wqi", [P, D], I32)
            g0s = sb2("g0s", [P, D], F32)
            g1s = sb2("g1s", [P, MT * C], F32)

            for ap_, h in [(p_t, p_hbm), (iota_bc, iota_bc_h), (iota_rev, iota_rev_h),
                           (expo, expo_h), (binm, binm_h), (ident, ident_h),
                           (bfrow_f, b_hbm["bf"][None, :])]:
                cnt.inc(nc.sync.dma_start(ap_[:, :], h), "ld", 16)
            ld_consts = cnt.val["ld"]

            cnt.inc(nc.vector.memset(ones_row[:, :], 1.0), "v")
            cnt.inc(nc.vector.memset(ones_f32[:, :], 1.0), "v")
            cnt.inc(nc.vector.memset(eps_t[:, :], LN_EPS), "v")
            cnt.inc(nc.vector.memset(S_f[:, :], 0.0), "v")
            cnt.inc(nc.vector.memset(S_b[:, :], 0.0), "v")

            twait("ld", ld_consts)
            cnt.inc(nc.tensor.matmul(psA[:, :1], ones_f32[:1, :], p_t[:1, :1],
                                     start=True, stop=True), "pe")
            swait("pe")
            cnt.inc(nc.scalar.activation(logl[:, 1:2], psA[:, :1], AF.Sigmoid), "s")
            cnt.inc(nc.scalar.activation(logl[:, 0:1], logl[:, 1:2], AF.Ln), "s")
            vwait("s")
            cnt.inc(nc.vector.tensor_scalar(
                logl[:, 0:1], logl[:, 0:1], -1e30, None, A.max), "v")
            swait("v")
            cnt.inc(nc.scalar.activation(lp_bc[:, :], iota_bc[:, :], AF.Exp,
                                         scale=logl[:, :1]), "s")
            cnt.inc(nc.scalar.activation(lrev[:, :], iota_rev[:, :], AF.Exp,
                                         scale=logl[:, :1]), "s")
            cnt.inc(nc.scalar.activation(lamC[:, :], logl[:, :1], AF.Exp,
                                         scale=float(C)), "s")
            vwait("s")
            cnt.inc(nc.vector.tensor_scalar(g0s[:, :MT * C], expo[:, :], logl[:, :1],
                                            0.0, A.mult, A.min), "v")
            swait("v")
            cnt.inc(nc.scalar.activation(g1s[:, :], g0s[:, :MT * C], AF.Exp), "s")
            vwait("s")
            cnt.inc(nc.vector.tensor_tensor(
                maskt[:, :], g1s[:, :], binm[:, :], A.mult), "v")

            for a in range(KT):
                ywait("v")
                cnt.inc(nc.sync.dma_start(
                    wf_st[:, :], wf_hbm[a * P:(a + 1) * P, :]), "ld", 16)
                vwait("ld")
                cnt.inc(nc.vector.tensor_reduce(
                    wsq[:, a:a + 1], wf_st[:, :], AX.X, A.max,
                    apply_absolute_value=True), "v")
                cnt.wait(nc.vector, "v", "v")
                cnt.inc(nc.vector.tensor_scalar(
                    wsq[:, a:a + 1], wsq[:, a:a + 1], 1e-8, QMAX_INV,
                    A.max, A.mult), "v")
                cnt.wait(nc.vector, "v", "v")
                cnt.inc(nc.vector.reciprocal(wrec[:, a:a + 1], wsq[:, a:a + 1]), "v")
                cnt.wait(nc.vector, "v", "v")
                cnt.inc(nc.vector.tensor_scalar(
                    wqi[:, :], wf_st[:, :], wrec[:, a:a + 1], None, A.mult), "v")
                cnt.inc(nc.vector.tensor_copy(
                    wfq_b[:, a * D:(a + 1) * D], wqi[:, :]), "v")
            v_wfq = cnt.val["v"]

            ywait("v", v_wfq)
            for a in range(KT):
                out_ap = bass.AP(wfT, a * P, [[KT * D, P], [D, KT], [1, P]])
                cnt.inc(nc.sync.dma_start_transpose(
                    out_ap, wfq_b[:, a * D:(a + 1) * D]), "trw", 16)

            twait("v", v_wfq)
            for a in range(KT):
                twait("v")
                cnt.inc(nc.tensor.transpose(
                    psA[:1, :P], wsq[:, a:a + 1], ident[:, :]), "pe")
                vwait("pe")
                cnt.inc(nc.vector.tensor_copy(
                    g0s[:1, a * P:(a + 1) * P], psA[:1, :P]), "v")
            for h in range(2):
                twait("v")
                cnt.inc(nc.tensor.matmul(psA[:, :], ones_f32[:1, :],
                                         g0s[:1, h * 512:(h + 1) * 512],
                                         start=True, stop=True), "pe")
                vwait("pe")
                cnt.inc(nc.vector.tensor_copy(
                    wsc_bc[:, h * 512:(h + 1) * 512], psA[:, :]), "v")
            for h in range(2):
                twait("v")
                cnt.inc(nc.tensor.matmul(psA[:, :], ones_f32[:1, :],
                                         bfrow_f[:1, h * 512:(h + 1) * 512],
                                         start=True, stop=True), "pe")
                vwait("pe")
                cnt.inc(nc.vector.tensor_copy(
                    bf_bc[:, h * 512:(h + 1) * 512], psA[:, :]), "v")
            cnt.inc(nc.vector.tensor_reduce(
                sbf_t[:, :], bf_bc[:, :], AX.X, A.add), "v")
            vflush()

        setup = {k: cnt.val[k] for k in ["v", "s", "trw", "ld", "pe"]}

        xb = [sb(f"xb{i}", [P, MT * D], F32) for i in range(2)]
        xn1b = sb("xn1b", [P, MT * D], BF16)
        xn1T = sb("xn1T", [P, KT * C], BF16, is_transpose_dest=True)
        qT = sb("qT", [P, KT * C], BF16)
        qTs = sb("qTs", [P, KT * C], BF16)
        k_tm = sb("k_tm", [P, MT * D], BF16)
        kTb = sb("kTb", [P, KT * C], BF16, is_transpose_dest=True)
        v_tm = sb("v_tm", [P, MT * D], BF16)
        at_b = sb("at_b", [P, MT * C], BF16)
        rdT = sb("rdT", [P, KT * C], BF16)
        G0 = sb("G0", [P, D], F32)
        G1 = sb("G1", [P, D], F32)
        I1 = sb("I1", [P, 512], I32)
        xqb = sb("xqb", [P, D], BF16)
        xqT = sb("xqT", [P, D], BF16, is_transpose_dest=True)

        for sn, val in setup.items():
            vwait(sn, val)
            swait(sn, val)
            ywait(sn, val)
            twait(sn, val)
            gwait(sn, val)

        def load_x(c, st_req=0):
            cnt.wait(nc.scalar, "s", "st", st_req)
            cnt.inc(nc.scalar.dma_start(
                xb[c % 2][:, :].rearrange("p (a d) -> p a d", a=MT),
                xin[c * C:(c + 1) * C, :].rearrange("(a p) d -> p a d", p=P)),
                "ld", 16)

        load_x(0)
        st_guard = 0
        s_sb_prev = 0

        def ln_stats(src_ap, junk_ap, mt):
            cnt.inc(nc.vector.tensor_reduce(
                STt[:, S_SUM + mt:S_SUM + mt + 1], src_ap, AX.X, A.add), "v")
            vflush()
            cnt.inc(nc.vector.tensor_scalar(
                STt[:, S_NEGM + mt:S_NEGM + mt + 1],
                STt[:, S_SUM + mt:S_SUM + mt + 1], -INV_D, None, A.mult), "v")
            swait("v")
            cnt.inc(nc.scalar.activation(
                junk_ap, src_ap,
                AF.Square, bias=STt[:, S_NEGM + mt:S_NEGM + mt + 1],
                accum_out=STt[:, S_SSQ + mt:S_SSQ + mt + 1]), "s")
            sflush()
            cnt.inc(nc.scalar.activation(
                STt[:, S_LNT + mt:S_LNT + mt + 1],
                STt[:, S_SSQ + mt:S_SSQ + mt + 1], AF.Ln,
                scale=INV_D, bias=eps_t[:, :1]), "s")
            cnt.inc(nc.scalar.activation(
                STt[:, S_INV + mt:S_INV + mt + 1],
                STt[:, S_LNT + mt:S_LNT + mt + 1], AF.Exp, scale=-0.5), "s")
            vwait("s")

        for c in range(nch):
            xc = xb[c % 2]
            x_ld = setup["ld"] + 16 * (c + 1)
            if c + 1 < nch:
                load_x(c + 1, st_guard)

            vwait("ld", x_ld)
            swait("ld", x_ld)
            for mt in range(MT):
                ln_stats(xc[:, mt * D:(mt + 1) * D], G1[:, :], mt)
                cnt.inc(nc.vector.tensor_scalar(
                    xn1b[:, mt * D:(mt + 1) * D], xc[:, mt * D:(mt + 1) * D],
                    STt[:, S_NEGM + mt:S_NEGM + mt + 1],
                    STt[:, S_INV + mt:S_INV + mt + 1], A.add, A.mult), "v")
            v_xn1 = cnt.val["v"]

            ywait("v", v_xn1)
            for mt in range(MT):
                out_ap = bass.AP(xn1T, mt * P, [[KT * C, P], [C, KT], [1, P]])
                cnt.inc(nc.sync.dma_start_transpose(
                    out_ap, xn1b[:, mt * D:(mt + 1) * D]), "tr", 16)
            tr_xn1 = cnt.val["tr"]

            twait("tr", tr_xn1)
            for g in range(4):
                bank, bname = (psA, "A") if g % 2 == 0 else (psB, "B")
                acquire(bname)
                for mi in range(2):
                    m = 2 * g + mi
                    for kk in range(KT):
                        mm = nc.tensor.matmul(
                            bank[:, mi * C:(mi + 1) * C],
                            ws["wq"][:, kk * D + m * P: kk * D + (m + 1) * P],
                            xn1T[:, kk * C:(kk + 1) * C],
                            start=(kk == 0), stop=(kk == KT - 1))
                cnt.inc(mm, "pe")
                swait("pe")
                for mi in range(2):
                    m = 2 * g + mi
                    cnt.inc(nc.scalar.activation(
                        qT[:, m * C:(m + 1) * C], bank[:, mi * C:(mi + 1) * C],
                        AF.Identity, bias=bq_pt[:, m:m + 1]), "s")
                release(bname, "s")
            s_qev = cnt.val["s"]
            vwait("s", s_qev)
            for kk in range(KT):
                cnt.inc(nc.vector.tensor_tensor(
                    qTs[:, kk * C:(kk + 1) * C], qT[:, kk * C:(kk + 1) * C],
                    lp_bc[:, :], A.mult), "v")

            for mt in range(MT):
                bank, bname = (psE, "E") if mt == 0 else (psF, "F")
                acquire(bname)
                for h in range(2):
                    for kk in range(KT):
                        mm = nc.tensor.matmul(
                            bank[:, h * 512:(h + 1) * 512],
                            xn1T[:, kk * C + mt * P: kk * C + (mt + 1) * P],
                            ws["wk"][:, kk * D + h * 512: kk * D + (h + 1) * 512],
                            start=(kk == 0), stop=False)
                    mm = nc.tensor.matmul(
                        bank[:, h * 512:(h + 1) * 512], ones_row[:1, :P],
                        brows[0:1, h * 512:(h + 1) * 512],
                        start=False, stop=True)
                cnt.inc(mm, "pe")
                vwait("pe")
                vwait("s")
                cnt.inc(nc.vector.tensor_scalar(
                    G0[:, :], bank[:, :], 0.0, None, A.min), "v")
                swait("v")
                cnt.inc(nc.scalar.activation(G1[:, :], G0[:, :], AF.Exp), "s")
                vwait("s")
                cnt.inc(nc.vector.scalar_tensor_tensor(
                    k_tm[:, mt * D:(mt + 1) * D], bank[:, :], 0.0,
                    G1[:, :], A.max, A.add), "v")
                release(bname, "v")
            v_ktm = cnt.val["v"]

            ywait("v", v_ktm)
            for mt in range(MT):
                out_ap = bass.AP(kTb, mt * P, [[KT * C, P], [C, KT], [1, P]])
                cnt.inc(nc.sync.dma_start_transpose(
                    out_ap, k_tm[:, mt * D:(mt + 1) * D]), "tr", 16)
            tr_k = cnt.val["tr"]
            vwait("tr", tr_k)
            for mt in range(MT):
                cnt.inc(nc.vector.tensor_scalar(
                    k_tm[:, mt * D:(mt + 1) * D], k_tm[:, mt * D:(mt + 1) * D],
                    lrev[:, mt:mt + 1], None, A.mult), "v")
            v_klam = cnt.val["v"]

            for i, (bank, bname) in enumerate(
                    [(psA, "A"), (psB, "B"), (psC, "C"), (psD, "D")]):
                mt, h = divmod(i, 2)
                acquire(bname)
                for kk in range(KT):
                    mm = nc.tensor.matmul(
                        bank[:, :],
                        xn1T[:, kk * C + mt * P: kk * C + (mt + 1) * P],
                        ws["wv"][:, kk * D + h * 512: kk * D + (h + 1) * 512],
                        start=(kk == 0), stop=False)
                mm = nc.tensor.matmul(
                    bank[:, :], ones_row[:1, :P],
                    brows[32:33, h * 512:(h + 1) * 512], start=False, stop=True)
                cnt.inc(mm, "pe")
                vwait("pe")
                cnt.inc(nc.vector.tensor_copy(
                    v_tm[:, mt * D + h * 512: mt * D + (h + 1) * 512],
                    bank[:, :]), "v")
                release(bname, "v")
            v_vtm = cnt.val["v"]

            acquire("A")
            twait("tr", tr_k)
            twait("s", s_qev)
            for mtk in range(MT):
                for kk in range(KT):
                    mm = nc.tensor.matmul(
                        psA[:, mtk * 256:(mtk + 1) * 256],
                        kTb[:, kk * C + mtk * P: kk * C + (mtk + 1) * P],
                        qT[:, kk * C:(kk + 1) * C],
                        start=(kk == 0), stop=(kk == KT - 1))
            cnt.inc(mm, "pe")
            vwait("pe")
            cnt.inc(nc.vector.tensor_tensor(
                at_b[:, :], psA[:, :], maskt[:, :], A.mult), "v")
            release("A", "v")
            v_atb = cnt.val["v"]

            twait("v", v_atb)
            twait("s", s_sb_prev)
            for gi, (bank, bname) in enumerate(
                    [(psB, "B"), (psC, "C"), (psD, "D"), (psE, "E")]):
                acquire(bname)
                for half in range(2):
                    j = gi * 2 + half
                    tgt = bank[:, half * C:(half + 1) * C]
                    for a in range(MT):
                        mm = nc.tensor.matmul(
                            tgt, v_tm[:, a * D + j * P: a * D + (j + 1) * P],
                            at_b[:, a * C:(a + 1) * C],
                            start=(a == 0), stop=False)
                    for kk in range(KT):
                        mm = nc.tensor.matmul(
                            tgt, S_b[:, kk * D + j * P: kk * D + (j + 1) * P],
                            qTs[:, kk * C:(kk + 1) * C],
                            start=False, stop=(kk == KT - 1))
                cnt.inc(mm, "pe")
                swait("pe")
                for half in range(2):
                    j = gi * 2 + half
                    cnt.inc(nc.scalar.activation(
                        rdT[:, j * C:(j + 1) * C],
                        bank[:, half * C:(half + 1) * C], AF.Copy), "s")
                release(bname, "s")
            s_rdT = cnt.val["s"]

            twait("v", v_klam)
            vwait("s", s_sb_prev)
            for i in range(KT):
                bank, bname = (psF, "F") if i % 2 == 0 else (psE, "E")
                acquire(bname)
                for h in range(2):
                    for a in range(MT):
                        mm = nc.tensor.matmul(
                            bank[:, h * 512:(h + 1) * 512],
                            k_tm[:, a * D + i * P: a * D + (i + 1) * P],
                            v_tm[:, a * D + h * 512: a * D + (h + 1) * 512],
                            start=(a == 0), stop=(a == MT - 1))
                cnt.inc(mm, "pe")
                vwait("pe")
                cnt.inc(nc.vector.scalar_tensor_tensor(
                    S_f[:, i * D:(i + 1) * D], S_f[:, i * D:(i + 1) * D],
                    lamC[:, :1], bank[:, :], A.mult, A.add), "v")
                release(bname, "v")
            v_supd = cnt.val["v"]
            swait("v", v_supd)
            for i in range(KT):
                cnt.inc(nc.scalar.activation(
                    S_b[:, i * D:(i + 1) * D], S_f[:, i * D:(i + 1) * D],
                    AF.Copy), "s")
            s_sb_prev = cnt.val["s"]

            twait("s", s_rdT)
            for mt in range(MT):
                bank, bname = (psF, "F") if mt == 0 else (psE, "E")
                acquire(bname)
                for h in range(2):
                    for kk in range(KT):
                        mm = nc.tensor.matmul(
                            bank[:, h * 512:(h + 1) * 512],
                            rdT[:, kk * C + mt * P: kk * C + (mt + 1) * P],
                            ws["wo"][:, kk * D + h * 512: kk * D + (h + 1) * 512],
                            start=(kk == 0), stop=False)
                    mm = nc.tensor.matmul(
                        bank[:, h * 512:(h + 1) * 512], ones_row[:1, :P],
                        brows[64:65, h * 512:(h + 1) * 512],
                        start=False, stop=True)
                cnt.inc(mm, "pe")
                vwait("pe")
                vwait("s")
                cnt.inc(nc.vector.tensor_copy(G0[:, :], bank[:, :]), "v")
                release(bname, "v")
                ln_stats(G0[:, :], G1[:, :], mt)
                cnt.inc(nc.vector.tensor_scalar(
                    G1[:, :], G0[:, :],
                    STt[:, S_NEGM + mt:S_NEGM + mt + 1],
                    STt[:, S_INV + mt:S_INV + mt + 1], A.add, A.mult), "v")
                cnt.inc(nc.vector.tensor_tensor(
                    xc[:, mt * D:(mt + 1) * D], G1[:, :],
                    xc[:, mt * D:(mt + 1) * D], A.add), "v")

            tr_base = cnt.val["tr"]
            for mt in range(MT):
                bank, bname = (psF, "F") if mt == 0 else (psE, "E")
                vwait("s")
                ln_stats(xc[:, mt * D:(mt + 1) * D], G1[:, :], mt)
                cnt.inc(nc.vector.tensor_scalar(
                    G0[:, :], xc[:, mt * D:(mt + 1) * D],
                    STt[:, S_NEGM + mt:S_NEGM + mt + 1],
                    STt[:, S_INV + mt:S_INV + mt + 1], A.add, A.mult), "v")
                cnt.inc(nc.vector.tensor_reduce(
                    STt[:, S_XS + mt:S_XS + mt + 1], G0[:, :],
                    AX.X, A.max, apply_absolute_value=True), "v")
                vflush()
                cnt.inc(nc.vector.tensor_scalar(
                    STt[:, S_XS + mt:S_XS + mt + 1],
                    STt[:, S_XS + mt:S_XS + mt + 1], 1e-8, QMAX_INV,
                    A.max, A.mult), "v")
                vflush()
                cnt.inc(nc.vector.reciprocal(
                    STt[:, S_REC + mt:S_REC + mt + 1],
                    STt[:, S_XS + mt:S_XS + mt + 1]), "v")
                vflush()
                vwait("tr", tr_base + 16 * mt)
                for h in range(2):
                    cnt.inc(nc.vector.tensor_scalar(
                        I1[:, :], G0[:, h * 512:(h + 1) * 512],
                        STt[:, S_REC + mt:S_REC + mt + 1], None, A.mult), "v")
                    cnt.inc(nc.vector.tensor_copy(
                        xqb[:, h * 512:(h + 1) * 512], I1[:, :]), "v")
                v_xq = cnt.val["v"]
                ywait("v", v_xq)
                out_ap = bass.AP(xqT, 0, [[D, P], [P, KT], [1, P]])
                cnt.inc(nc.sync.dma_start_transpose(out_ap, xqb[:, :]), "tr", 16)
                tr_xq = cnt.val["tr"]

                twait("tr", tr_xq)
                acquire(bname)
                for h in range(2):
                    for kk in range(KT):
                        mm = nc.tensor.matmul(
                            bank[:, h * 512:(h + 1) * 512],
                            xqT[:, kk * P:(kk + 1) * P],
                            wfT[:, kk * D + h * 512: kk * D + (h + 1) * 512],
                            start=(kk == 0), stop=(kk == KT - 1))
                cnt.inc(mm, "pe")
                vwait("pe")
                cnt.inc(nc.vector.scalar_tensor_tensor(
                    G1[:, :], bank[:, :],
                    STt[:, S_XS + mt:S_XS + mt + 1], wsc_bc[:, :],
                    A.mult, A.mult), "v")
                release(bname, "v")
                cnt.inc(nc.vector.tensor_tensor(
                    G0[:, :], G1[:, :], bf_bc[:, :], A.add), "v")
                ln_stats(G0[:, :], G1[:, :], mt)
                cnt.inc(nc.vector.tensor_scalar(
                    G1[:, :], G0[:, :],
                    STt[:, S_NEGM + mt:S_NEGM + mt + 1],
                    STt[:, S_INV + mt:S_INV + mt + 1], A.add, A.mult), "v")
                cnt.inc(nc.vector.tensor_tensor(
                    xc[:, mt * D:(mt + 1) * D], G1[:, :],
                    xc[:, mt * D:(mt + 1) * D], A.add), "v")
                v_out = cnt.val["v"]
                swait("v", v_out)
                cnt.inc(nc.scalar.dma_start(
                    yout[c * C + mt * P: c * C + (mt + 1) * P, :],
                    xc[:, mt * D:(mt + 1) * D]), "st", 16)
            st_guard = cnt.val["st"]

        if taps:
            local = {"xn1b": xn1b, "qT": qT, "qTs": qTs, "k_tm": k_tm,
                     "kTb": kTb, "v_tm": v_tm, "at_b": at_b, "rdT": rdT,
                     "S_f": S_f, "xqb": xqb, "maskt": maskt, "lp_bc": lp_bc,
                     "STt": STt}
            ywait("v")
            ywait("s")
            for nm, ap_h in dbg.items():
                src = local[nm]
                wtot = ap_h.shape[1]
                for off in range(0, wtot, D):
                    wd = min(D, wtot - off)
                    ywait("v")
                    cnt.inc(nc.vector.tensor_copy(G0[:, :wd],
                                                  src[:, off:off + wd]), "v")
                    ywait("v")
                    cnt.inc(nc.sync.dma_start(ap_h[:, off:off + wd],
                                              G0[:, :wd]), "st", 16)
                    ywait("st")
                    cnt.wait(nc.vector, "v", "st", cnt.val["st"])
        nc.sync.wait_ge(cnt.st, cnt.val["st"])
    return nc


def _consts():
    iota_bc = np.broadcast_to(
        np.arange(C, dtype=np.float32)[None, :], (P, C)).copy()
    iota_rev = np.empty((P, MT), np.float32)
    for mt in range(MT):
        iota_rev[:, mt] = C - 1 - (mt * P + np.arange(P))
    expo = np.empty((P, MT * C), np.float32)
    binm = np.empty((P, MT * C), np.float32)
    for a in range(MT):
        j = a * P + np.arange(P)[:, None]
        i = np.arange(C)[None, :]
        expo[:, a * C:(a + 1) * C] = (i - 1 - j).astype(np.float32)
        binm[:, a * C:(a + 1) * C] = (j < i).astype(np.float32)
    ident = np.eye(P, dtype=np.float32)
    return iota_bc, iota_rev, expo, binm, ident


def kernel_v1(L0_attn_x, L0_attn_Wq, L0_attn_bq, L0_attn_Wk, L0_attn_bk,
           L0_attn_Wv, L0_attn_bv, L0_attn_Wo, L0_attn_bo,
           L0_attn_decay_param, L0_ffn_weight, L0_ffn_bias):
    x = np.ascontiguousarray(np.asarray(L0_attn_x, dtype=np.float32))
    B = x.shape[0]
    iota_bc, iota_rev, expo, binm, ident = _consts()
    common = {
        "wq": np.ascontiguousarray(L0_attn_Wq, dtype=np.float32),
        "wk": np.ascontiguousarray(L0_attn_Wk, dtype=np.float32),
        "wv": np.ascontiguousarray(L0_attn_Wv, dtype=np.float32),
        "wo": np.ascontiguousarray(L0_attn_Wo, dtype=np.float32),
        "wf": np.ascontiguousarray(L0_ffn_weight, dtype=np.float32),
        "bq": np.ascontiguousarray(L0_attn_bq, dtype=np.float32),
        "bk": np.ascontiguousarray(L0_attn_bk, dtype=np.float32),
        "bv": np.ascontiguousarray(L0_attn_bv, dtype=np.float32),
        "bo": np.ascontiguousarray(L0_attn_bo, dtype=np.float32),
        "bf": np.ascontiguousarray(L0_ffn_bias, dtype=np.float32),
        "decay": np.asarray(L0_attn_decay_param, dtype=np.float32).reshape(1, 1),
        "iota_bc": iota_bc, "iota_rev": iota_rev,
        "expo": expo, "binm": binm, "ident": ident,
    }
    nc = build_nc()
    in_maps = [dict(common, xin=x[b]) for b in range(B)]
    res = run_bass_kernel_spmd(nc, in_maps, core_ids=list(range(B)))
    out = np.stack([np.asarray(res.results[b]["yout"]) for b in range(B)], axis=0)
    return out.astype(np.float32)


TH = 4096
NC2 = TH // C


def build_nc2(nb=4, nch_b=NC2, taps=False):
    nc = bass.Bass("TRN2", target_bir_lowering=False)

    xin = nc.dram_tensor("xin", [TH, D], F32, kind="ExternalInput").ap()
    w_hbm = {nm: nc.dram_tensor(nm, [D, D], F32, kind="ExternalInput").ap()
             for nm in ["wq", "wk", "wv", "wo"]}
    wf_hbm = nc.dram_tensor("wf", [D, D], F32, kind="ExternalInput").ap()
    b_hbm = {nm: nc.dram_tensor(nm, [D], F32, kind="ExternalInput").ap()
             for nm in ["bq", "bk", "bv", "bo", "bf"]}
    p_hbm = nc.dram_tensor("decay", [1, 1], F32, kind="ExternalInput").ap()
    iota_bc_h = nc.dram_tensor("iota_bc", [P, C], F32, kind="ExternalInput").ap()
    iota_rev_h = nc.dram_tensor("iota_rev", [P, MT], F32, kind="ExternalInput").ap()
    expo_h = nc.dram_tensor("expo", [P, MT * C], F32, kind="ExternalInput").ap()
    binm_h = nc.dram_tensor("binm", [P, MT * C], F32, kind="ExternalInput").ap()
    ident_h = nc.dram_tensor("ident", [P, P], F32, kind="ExternalInput").ap()
    iota_cc_h = nc.dram_tensor("iota_cc", [P, NC2], F32, kind="ExternalInput").ap()
    rflag_h = nc.dram_tensor("rflag", [P, 1], F32, kind="ExternalInput").ap()
    sflag_h = nc.dram_tensor("sflag", [P, 1], F32, kind="ExternalInput").ap()
    yout = nc.dram_tensor("yout", [TH, D], F32, kind="ExternalOutput").ap()
    dbg = {}
    if taps:
        for nm, shape in [("qTs_l", [P, KT * C]), ("at_l", [P, MT * C]),
                          ("vt_l", [P, MT * D]), ("Sc0", [P, KT * D]),
                          ("rdT", [P, KT * C]), ("M_r", [P, KT * D]),
                          ("qTg", [P, KT * C]), ("xb0", [P, MT * D]),
                          ("xb1", [P, MT * D])]:
            dbg[nm] = nc.dram_tensor("dbg_" + nm, shape, F32,
                                     kind="ExternalOutput").ap()

    sp_qTs = nc.dram_tensor("sp_qTs", [NC2 * P, KT * C], BF16, kind="Internal").ap()
    sp_at = nc.dram_tensor("sp_at", [NC2 * P, MT * C], BF16, kind="Internal").ap()
    sp_vt = nc.dram_tensor("sp_vt", [NC2 * P, MT * D], BF16, kind="Internal").ap()
    sp_S = nc.dram_tensor("sp_S", [(NC2 + 1) * P, KT * D], BF16, kind="Internal").ap()
    cc_in = nc.dram_tensor("cc_in", [P, KT * D], BF16, kind="Internal")
    cc_out = nc.dram_tensor("cc_out", [P, KT * D], BF16, kind="Internal")

    with ExitStack() as es:
        def sb(name, shape, dtype, **kw):
            return es.enter_context(nc.sbuf_tensor(name, shape, dtype, **kw))

        cnt = Counts(nc)
        for s_ in ["ld", "st", "pe", "v", "s", "cc",
                   "trx", "trk", "trq", "trw",
                   "spq", "spa", "spv", "sps", "spc",
                   "ldx0", "ldx1", "lds0", "lds1", "ldsm", "ldm2", "ldat",
                   "st0", "st1"]:
            cnt.sem(s_, es)
        LT = {}

        def twait(sn, v=None):
            cnt.wait(nc.tensor, "t", sn, v)

        def vwait(sn, v=None):
            cnt.wait(nc.vector, "v", sn, v)

        def swait(sn, v=None):
            cnt.wait(nc.scalar, "s", sn, v)

        def gwait(sn, v=None):
            cnt.wait(nc.gpsimd, "g", sn, v)

        def ywait(sn, v=None):
            cnt.wait(nc.sync, "y", sn, v)

        def vflush():
            cnt.wait(nc.vector, "v", "v")

        def sflush():
            cnt.wait(nc.scalar, "s", "s")

        ws = {nm: sb(nm + "_s", [P, KT * D], BF16) for nm in ["wq", "wk", "wv", "wo"]}
        wfT = sb("wfT_s", [P, KT * D], BF16)
        bq_pt = sb("bq_pt", [P, KT], F32)
        brows = sb("brows", [P, D], BF16)
        brow = {"bk": brows[0:1, :], "bv": brows[32:33, :], "bo": brows[64:65, :]}
        ones_row = sb("ones_row", [P, C], BF16)
        ones_f32 = sb("ones_f32", [1, P], F32)
        wsc_bc = sb("wsc_bc", [P, D], F32)
        bf_bc = sb("bf_bc", [P, D], F32)
        maskt = sb("maskt", [P, MT * C], BF16)
        lp_bc = sb("lp_bc", [P, C], BF16)
        lrev = sb("lrev", [P, MT], F32)
        lamC = sb("lamC", [P, 1], F32)
        lamcc = sb("lamcc", [P, NC2], F32)
        logl = sb("logl", [P, 2], F32)
        STt = sb("stats", [P, 16], F32)
        eps_t = sb("eps_t", [P, 1], F32)
        sflag = sb("sflag_t", [P, 1], F32)
        sbf_t = sb("sbf_t", [P, 1], F32)

        psA = es.enter_context(nc.psum_tensor("psA", [P, 512], F32))
        psB = es.enter_context(nc.psum_tensor("psB", [P, 512], F32))
        psC = es.enter_context(nc.psum_tensor("psC", [P, 512], F32))
        psD = es.enter_context(nc.psum_tensor("psD", [P, 512], F32))
        psE = es.enter_context(nc.psum_tensor("psE", [P, 1024], F32))
        psF = es.enter_context(nc.psum_tensor("psF", [P, 1024], F32))
        bank_guard = {}

        def acquire(bname):
            g = bank_guard.get(bname)
            if g is not None:
                twait(g[0], g[1])

        def release(bname, sname):
            bank_guard[bname] = (sname, cnt.val[sname])

        for nm in ["wq", "wk", "wv", "wo"]:
            cnt.inc(nc.gpsimd.dma_start(
                ws[nm][:, :].rearrange("p (a n) -> p a n", a=KT),
                w_hbm[nm].rearrange("(a p) n -> p a n", p=P)), "ld", 16)
        cnt.inc(nc.gpsimd.dma_start(
            bq_pt[:, :], b_hbm["bq"].rearrange("(a p) -> p a", p=P),
            allow_slow_non_contiguous=True), "ld", 16)
        for nm in ["bk", "bv", "bo"]:
            cnt.inc(nc.gpsimd.dma_start(brow[nm], b_hbm[nm][None, :]), "ld", 16)

        with ExitStack() as es2:
            def sb2(name, shape, dtype, **kw):
                return es2.enter_context(nc.sbuf_tensor(name, shape, dtype, **kw))
            p_t = sb2("p_t", [1, 1], F32)
            iota_bc = sb2("iota_bc_s", [P, C], F32)
            iota_rev = sb2("iota_rev_s", [P, MT], F32)
            iota_cc = sb2("iota_cc_s", [P, NC2], F32)
            rflag = sb2("rflag_t", [P, 1], F32)
            expo = sb2("expo_s", [P, MT * C], F32)
            binm = sb2("binm_s", [P, MT * C], F32)
            ident = sb2("ident_s", [P, P], F32)
            bfrow_f = sb2("bfrow_f", [1, D], F32)
            wf_st = sb2("wf_st", [P, D], F32)
            wfq_b = sb2("wfq_b", [P, KT * D], BF16)
            wsq = sb2("wsq", [P, KT], F32)
            wrec = sb2("wrec", [P, KT], F32)
            wqi = sb2("wqi", [P, D], I32)
            g0s = sb2("g0s", [P, D], F32)
            g1s = sb2("g1s", [P, MT * C], F32)

            for ap_, h in [(p_t, p_hbm), (iota_bc, iota_bc_h), (iota_rev, iota_rev_h),
                           (expo, expo_h), (binm, binm_h), (ident, ident_h),
                           (bfrow_f, b_hbm["bf"][None, :]), (iota_cc, iota_cc_h),
                           (rflag, rflag_h), (sflag, sflag_h)]:
                cnt.inc(nc.sync.dma_start(ap_[:, :], h), "ld", 16)
            ld_consts = cnt.val["ld"]

            cnt.inc(nc.vector.memset(ones_row[:, :], 1.0), "v")
            cnt.inc(nc.vector.memset(ones_f32[:, :], 1.0), "v")
            cnt.inc(nc.vector.memset(eps_t[:, :], LN_EPS), "v")

            twait("ld", ld_consts)
            cnt.inc(nc.tensor.matmul(psA[:, :1], ones_f32[:1, :], p_t[:1, :1],
                                     start=True, stop=True), "pe")
            swait("pe")
            cnt.inc(nc.scalar.activation(logl[:, 1:2], psA[:, :1], AF.Sigmoid), "s")
            cnt.inc(nc.scalar.activation(logl[:, 0:1], logl[:, 1:2], AF.Ln), "s")
            vwait("s")
            cnt.inc(nc.vector.tensor_scalar(
                logl[:, 0:1], logl[:, 0:1], -1e30, None, A.max), "v")
            swait("v")
            cnt.inc(nc.scalar.activation(lp_bc[:, :], iota_bc[:, :], AF.Exp,
                                         scale=logl[:, :1]), "s")
            cnt.inc(nc.scalar.activation(lrev[:, :], iota_rev[:, :], AF.Exp,
                                         scale=logl[:, :1]), "s")
            cnt.inc(nc.scalar.activation(lamC[:, :], logl[:, :1], AF.Exp,
                                         scale=float(C)), "s")
            cnt.inc(nc.scalar.activation(lamcc[:, :], iota_cc[:, :], AF.Exp,
                                         scale=logl[:, :1]), "s")
            sflush()
            vwait("s")
            cnt.inc(nc.vector.tensor_scalar(
                lamcc[:, :], lamcc[:, :], rflag[:, :1], None, A.mult), "v")
            cnt.inc(nc.vector.tensor_scalar(g1s[:, :], expo[:, :], logl[:, :1],
                                            0.0, A.mult, A.min), "v")
            swait("v")
            cnt.inc(nc.scalar.activation(g1s[:, :], g1s[:, :], AF.Exp), "s")
            vwait("s")
            cnt.inc(nc.vector.tensor_tensor(
                maskt[:, :], g1s[:, :], binm[:, :], A.mult), "v")

            for a in range(KT):
                ywait("v")
                cnt.inc(nc.sync.dma_start(
                    wf_st[:, :], wf_hbm[a * P:(a + 1) * P, :]), "ld", 16)
                vwait("ld")
                cnt.inc(nc.vector.tensor_reduce(
                    wsq[:, a:a + 1], wf_st[:, :], AX.X, A.max,
                    apply_absolute_value=True), "v")
                vflush()
                cnt.inc(nc.vector.tensor_scalar(
                    wsq[:, a:a + 1], wsq[:, a:a + 1], 1e-8, QMAX_INV,
                    A.max, A.mult), "v")
                vflush()
                cnt.inc(nc.vector.reciprocal(wrec[:, a:a + 1], wsq[:, a:a + 1]), "v")
                vflush()
                cnt.inc(nc.vector.tensor_scalar(
                    wqi[:, :], wf_st[:, :], wrec[:, a:a + 1], None, A.mult), "v")
                cnt.inc(nc.vector.tensor_copy(
                    wfq_b[:, a * D:(a + 1) * D], wqi[:, :]), "v")
            v_wfq = cnt.val["v"]

            ywait("v", v_wfq)
            for a in range(KT):
                out_ap = bass.AP(wfT, a * P, [[KT * D, P], [D, KT], [1, P]])
                cnt.inc(nc.sync.dma_start_transpose(
                    out_ap, wfq_b[:, a * D:(a + 1) * D]), "trw", 16)

            twait("v", v_wfq)
            for a in range(KT):
                twait("v")
                cnt.inc(nc.tensor.transpose(
                    psA[:1, :P], wsq[:, a:a + 1], ident[:, :]), "pe")
                vwait("pe")
                cnt.inc(nc.vector.tensor_copy(
                    g0s[:1, a * P:(a + 1) * P], psA[:1, :P]), "v")
            for h in range(2):
                twait("v")
                cnt.inc(nc.tensor.matmul(psA[:, :], ones_f32[:1, :],
                                         g0s[:1, h * 512:(h + 1) * 512],
                                         start=True, stop=True), "pe")
                vwait("pe")
                cnt.inc(nc.vector.tensor_copy(
                    wsc_bc[:, h * 512:(h + 1) * 512], psA[:, :]), "v")
            for h in range(2):
                twait("v")
                cnt.inc(nc.tensor.matmul(psA[:, :], ones_f32[:1, :],
                                         bfrow_f[:1, h * 512:(h + 1) * 512],
                                         start=True, stop=True), "pe")
                vwait("pe")
                cnt.inc(nc.vector.tensor_copy(
                    bf_bc[:, h * 512:(h + 1) * 512], psA[:, :]), "v")
            cnt.inc(nc.vector.tensor_reduce(
                sbf_t[:, :], bf_bc[:, :], AX.X, A.add), "v")
            vflush()

        setup = {k: cnt.val[k] for k in ["v", "s", "trw", "ld", "pe"]}

        def barrier(vals):
            for sn, val in vals.items():
                vwait(sn, val)
                swait(sn, val)
                ywait(sn, val)
                twait(sn, val)
                gwait(sn, val)

        def ln_tail(src0, src1, junk, ctr, add_sbf=False):
            vflush()
            if add_sbf:
                cnt.inc(nc.vector.tensor_scalar(
                    STt[:, S_NEGM:S_NEGM + 2], STt[:, S_SUM:S_SUM + 2],
                    sbf_t[:, :1], -INV_D, A.add, A.mult), "v")
            else:
                cnt.inc(nc.vector.tensor_scalar(
                    STt[:, S_NEGM:S_NEGM + 2], STt[:, S_SUM:S_SUM + 2],
                    -INV_D, None, A.mult), "v")
            swait("v")
            cnt.inc(nc.scalar.activation(
                junk[:, 0:D], src0,
                AF.Square, bias=STt[:, S_NEGM:S_NEGM + 1],
                accum_out=STt[:, S_SSQ:S_SSQ + 1]), "s")
            cnt.inc(nc.scalar.activation(
                junk[:, 0:D], src1,
                AF.Square, bias=STt[:, S_NEGM + 1:S_NEGM + 2],
                accum_out=STt[:, S_SSQ + 1:S_SSQ + 2]), "s")
            sflush()
            cnt.inc(nc.scalar.activation(
                STt[:, S_LNT:S_LNT + 2], STt[:, S_SSQ:S_SSQ + 2], AF.Ln,
                scale=INV_D, bias=eps_t[:, :1]), "s")
            cnt.inc(nc.scalar.activation(
                STt[:, S_INV:S_INV + 2], STt[:, S_LNT:S_LNT + 2], AF.Exp,
                scale=-0.5), "s")
            vflush()
            cnt.inc(nc.vector.tensor_scalar(
                ctr[:, 0:D], src0, STt[:, S_NEGM:S_NEGM + 1], None, A.add), "v")
            cnt.inc(nc.vector.tensor_scalar(
                ctr[:, D:2 * D], src1, STt[:, S_NEGM + 1:S_NEGM + 2], None,
                A.add), "v")

        def ln_center2(src0, src1, junk, ctr):
            cnt.inc(nc.vector.tensor_reduce(
                STt[:, S_SUM:S_SUM + 1], src0, AX.X, A.add), "v")
            cnt.inc(nc.vector.tensor_reduce(
                STt[:, S_SUM + 1:S_SUM + 2], src1, AX.X, A.add), "v")
            ln_tail(src0, src1, junk, ctr)

        with ExitStack() as esA:
            def sbA(name, shape, dtype, **kw):
                return esA.enter_context(nc.sbuf_tensor(name, shape, dtype, **kw))
            S_f = sbA("S_f", [P, KT * D], F32)
            S_b = sbA("S_b", [P, KT * D], BF16)
            xbA = [sbA(f"xbA{i}", [P, MT * D], F32) for i in range(2)]
            xn1b = sbA("xn1b", [P, MT * D], BF16)
            xn1T = sbA("xn1T", [P, KT * C], BF16, is_transpose_dest=True)
            qT = sbA("qT", [P, KT * C], BF16)
            qTs = sbA("qTs", [P, KT * C], BF16)
            k_tm = sbA("k_tm", [P, MT * D], BF16)
            k_lm = sbA("k_lm", [P, MT * D], BF16)
            kTb = sbA("kTb", [P, KT * C], BF16, is_transpose_dest=True)
            v_tm = sbA("v_tm", [P, MT * D], BF16)
            at_b = sbA("at_b", [P, MT * C], BF16)
            G0 = sbA("G0a", [P, MT * D], F32)
            G1 = sbA("G1a", [P, MT * D], F32)

            barrier(setup)
            cnt.inc(nc.vector.memset(S_f[:, :], 0.0), "v")
            cnt.inc(nc.vector.memset(S_b[:, :], 0.0), "v")
            vz = cnt.val["v"]
            cnt.wait(nc.scalar, "s", "v", vz)
            cnt.inc(nc.scalar.dma_start(sp_S[0:P, :], S_b[:, :]), "sps", 16)

            def load_xA(c):
                sm = "ldx0" if c % 2 == 0 else "ldx1"
                cnt.inc(nc.gpsimd.dma_start(
                    xbA[c % 2][:, :].rearrange("p (a d) -> p a d", a=MT),
                    xin[c * C:(c + 1) * C, :].rearrange("(a p) d -> p a d", p=P)),
                    sm, 16)
                LT[("x", c)] = (sm, cnt.val[sm])

            load_xA(0)
            pe_vproj = [0] * (NC2 + 1)

            def emit_front(c):
                sm, tv = LT[("x", c)]
                xc = xbA[c % 2]
                vwait(sm, tv)
                swait(sm, tv)
                ln_center2(xc[:, 0:D], xc[:, D:2 * D], G0, G1)
                vwait("s")
                for mt in range(MT):
                    cnt.inc(nc.vector.tensor_scalar(
                        xn1b[:, mt * D:(mt + 1) * D], G1[:, mt * D:(mt + 1) * D],
                        STt[:, S_INV + mt:S_INV + mt + 1], None, A.mult), "v")
                v_xn1 = cnt.val["v"]
                ywait("v", v_xn1)
                if c > 0:
                    ywait("pe", pe_vproj[c - 1])
                for mt in range(MT):
                    out_ap = bass.AP(xn1T, mt * P, [[KT * C, P], [C, KT], [1, P]])
                    cnt.inc(nc.sync.dma_start_transpose(
                        out_ap, xn1b[:, mt * D:(mt + 1) * D]), "trx", 16)
                return cnt.val["trx"], cnt.val["s"]

            tr_front, s_front = emit_front(0)
            s_sb_prev = 0
            stt_v = [0] * KT

            for c in range(NC2):
                if c + 1 < NC2:
                    load_xA(c + 1)
                tr_xn1 = tr_front

                kb = [(psE, "E"), (psF, "F")]
                pe_k = [0, 0]
                for mt in range(MT):
                    twait("trx", tr_xn1 - 16 * (MT - 1 - mt))
                    bank, bname = kb[mt]
                    acquire(bname)
                    for h in range(2):
                        for kk in range(KT):
                            mm = nc.tensor.matmul(
                                bank[:, h * 512:(h + 1) * 512],
                                xn1T[:, kk * C + mt * P: kk * C + (mt + 1) * P],
                                ws["wk"][:, kk * D + h * 512: kk * D + (h + 1) * 512],
                                start=(kk == 0), stop=False)
                        mm = nc.tensor.matmul(
                            bank[:, h * 512:(h + 1) * 512], ones_row[0:1, :P],
                            brows[0:1, h * 512:(h + 1) * 512],
                            start=False, stop=True)
                    cnt.inc(mm, "pe")
                    pe_k[mt] = cnt.val["pe"]
                for mt in range(MT):
                    vwait("pe", pe_k[mt])
                    vwait("s", s_front)
                    cnt.inc(nc.vector.tensor_scalar(
                        G0[:, mt * D:(mt + 1) * D], kb[mt][0][:, :],
                        0.0, None, A.min), "v")
                swait("v")
                for mt in range(MT):
                    cnt.inc(nc.scalar.activation(
                        G1[:, mt * D:(mt + 1) * D], G0[:, mt * D:(mt + 1) * D],
                        AF.Exp), "s")
                vwait("s")
                vwait("trk")
                for mt in range(MT):
                    cnt.inc(nc.vector.scalar_tensor_tensor(
                        k_tm[:, mt * D:(mt + 1) * D], kb[mt][0][:, :], 0.0,
                        G1[:, mt * D:(mt + 1) * D], A.max, A.add), "v")
                    release(kb[mt][1], "v")
                v_ktm = cnt.val["v"]

                ywait("v", v_ktm)
                for mt in range(MT):
                    out_ap = bass.AP(kTb, mt * P, [[KT * C, P], [C, KT], [1, P]])
                    cnt.inc(nc.sync.dma_start_transpose(
                        out_ap, k_tm[:, mt * D:(mt + 1) * D]), "trk", 16)
                tr_k = cnt.val["trk"]
                for mt in range(MT):
                    cnt.inc(nc.vector.tensor_scalar(
                        k_lm[:, mt * D:(mt + 1) * D], k_tm[:, mt * D:(mt + 1) * D],
                        lrev[:, mt:mt + 1], None, A.mult), "v")
                v_klam = cnt.val["v"]

                for i, (bank, bname) in enumerate(
                        [(psA, "A"), (psB, "B"), (psC, "C"), (psD, "D")]):
                    mt, h = divmod(i, 2)
                    acquire(bname)
                    for kk in range(KT):
                        mm = nc.tensor.matmul(
                            bank[:, :],
                            xn1T[:, kk * C + mt * P: kk * C + (mt + 1) * P],
                            ws["wv"][:, kk * D + h * 512: kk * D + (h + 1) * 512],
                            start=(kk == 0), stop=False)
                    mm = nc.tensor.matmul(
                        bank[:, :], ones_row[32:33, :P],
                        brows[32:33, h * 512:(h + 1) * 512], start=False, stop=True)
                    cnt.inc(mm, "pe")
                    swait("pe")
                    if i == 0:
                        swait("spv")
                    cnt.inc(nc.scalar.activation(
                        v_tm[:, mt * D + h * 512: mt * D + (h + 1) * 512],
                        bank[:, :], AF.Copy), "s")
                    release(bname, "s")
                pe_vproj[c] = cnt.val["pe"]
                s_vtm = cnt.val["s"]
                gwait("s", s_vtm)
                cnt.inc(nc.gpsimd.dma_start(
                    sp_vt[c * P:(c + 1) * P, :], v_tm[:, :]), "spv", 16)

                for g in range(4):
                    bank, bname = (psA, "A") if g % 2 == 0 else (psB, "B")
                    acquire(bname)
                    for mi in range(2):
                        m = 2 * g + mi
                        for kk in range(KT):
                            mm = nc.tensor.matmul(
                                bank[:, mi * C:(mi + 1) * C],
                                ws["wq"][:, kk * D + m * P: kk * D + (m + 1) * P],
                                xn1T[:, kk * C:(kk + 1) * C],
                                start=(kk == 0), stop=(kk == KT - 1))
                    cnt.inc(mm, "pe")
                    swait("pe")
                    for mi in range(2):
                        m = 2 * g + mi
                        cnt.inc(nc.scalar.activation(
                            qT[:, m * C:(m + 1) * C], bank[:, mi * C:(mi + 1) * C],
                            AF.Identity, bias=bq_pt[:, m:m + 1]), "s")
                    release(bname, "s")
                s_qev = cnt.val["s"]
                vwait("s", s_qev)
                vwait("spq")
                for kk in range(KT):
                    cnt.inc(nc.vector.tensor_tensor(
                        qTs[:, kk * C:(kk + 1) * C], qT[:, kk * C:(kk + 1) * C],
                        lp_bc[:, :], A.mult), "v")
                v_qts = cnt.val["v"]
                gwait("v", v_qts)
                cnt.inc(nc.gpsimd.dma_start(
                    sp_qTs[c * P:(c + 1) * P, :], qTs[:, :]), "spq", 16)

                if c + 1 < NC2:
                    tr_front, s_front = emit_front(c + 1)

                acquire("A")
                twait("trk", tr_k)
                twait("s", s_qev)
                for mtk in range(MT):
                    for kk in range(KT):
                        mm = nc.tensor.matmul(
                            psA[:, mtk * 256:(mtk + 1) * 256],
                            kTb[:, kk * C + mtk * P: kk * C + (mtk + 1) * P],
                            qT[:, kk * C:(kk + 1) * C],
                            start=(kk == 0), stop=(kk == KT - 1))
                cnt.inc(mm, "pe")
                vwait("pe")
                vwait("spa")
                cnt.inc(nc.vector.tensor_tensor(
                    at_b[:, :], psA[:, :], maskt[:, :], A.mult), "v")
                release("A", "v")
                v_atb = cnt.val["v"]
                gwait("v", v_atb)
                cnt.inc(nc.gpsimd.dma_start(
                    sp_at[c * P:(c + 1) * P, :], at_b[:, :]), "spa", 16)

                twait("v", v_klam)
                vwait("s", s_sb_prev)
                for i in range(KT):
                    bank, bname = (psF, "F") if i % 2 == 0 else (psE, "E")
                    acquire(bname)
                    for h in range(2):
                        for a in range(MT):
                            mm = nc.tensor.matmul(
                                bank[:, h * 512:(h + 1) * 512],
                                k_lm[:, a * D + i * P: a * D + (i + 1) * P],
                                v_tm[:, a * D + h * 512: a * D + (h + 1) * 512],
                                start=(a == 0), stop=(a == MT - 1))
                    cnt.inc(mm, "pe")
                    vwait("pe")
                    cnt.inc(nc.vector.scalar_tensor_tensor(
                        S_f[:, i * D:(i + 1) * D], S_f[:, i * D:(i + 1) * D],
                        lamC[:, :1], bank[:, :], A.mult, A.add), "v")
                    release(bname, "v")
                    stt_v[i] = cnt.val["v"]
                v_supd = cnt.val["v"]
                swait("sps")
                for i in range(KT):
                    swait("v", stt_v[i])
                    cnt.inc(nc.scalar.activation(
                        S_b[:, i * D:(i + 1) * D], S_f[:, i * D:(i + 1) * D],
                        AF.Copy), "s")
                s_sb_prev = cnt.val["s"]
                gwait("s", s_sb_prev)
                cnt.inc(nc.gpsimd.dma_start(
                    sp_S[(c + 1) * P:(c + 2) * P, :], S_b[:, :]), "sps", 16)

            vwait("sps")
            cnt.inc(nc.vector.tensor_scalar(
                S_b[:, :], S_b[:, :], sflag[:, :1], None, A.mult), "v")
            gwait("v")
            cnt.inc(nc.gpsimd.dma_start(cc_in.ap(), S_b[:, :]), "spc", 16)
            gwait("spc")
            cnt.inc(nc.gpsimd.collective_compute(
                "AllReduce", A.add,
                replica_groups=[[2 * i, 2 * i + 1] for i in range(nb)],
                ins=[cc_in.ap()], outs=[cc_out.ap()]), "cc", 1)

            phA = {k: cnt.val[k] for k in ["v", "s", "trx", "trk", "pe",
                                           "ldx0", "ldx1",
                                           "spq", "spa", "spv", "sps", "spc"]}

        with ExitStack() as esB:
            def sbB(name, shape, dtype, **kw):
                return esB.enter_context(nc.sbuf_tensor(name, shape, dtype, **kw))
            M_r = sbB("M_r", [P, KT * D], BF16)
            Sc = [sbB(f"Sc{i}", [P, KT * D], BF16) for i in range(2)]
            xbB = [sbB(f"xbB{i}", [P, MT * D], F32) for i in range(2)]
            qTs_l = [sbB(f"qTs_l{i}", [P, KT * C], BF16) for i in range(2)]
            qTg = sbB("qTg", [P, KT * C], BF16)
            at_l = sbB("at_l", [P, MT * C], BF16)
            vt_l = sbB("vt_l", [P, MT * D], BF16)
            rdT = sbB("rdT", [P, KT * C], BF16)
            G0 = sbB("G0b", [P, MT * D], F32)
            G1 = sbB("G1b", [P, MT * D], F32)
            I1 = sbB("I1", [P, 512], I32)
            xqb = sbB("xqb", [P, MT * D], BF16)
            xqT = sbB("xqT", [P, MT * D], BF16, is_transpose_dest=True)

            barrier(phA)
            gwait("cc")
            cnt.inc(nc.gpsimd.dma_start(M_r[:, :], cc_out.ap()), "ldm2", 16)
            twait("ldm2", 16)

            ldv = {}

            def load_xB(c):
                sm = "ldx0" if c % 2 == 0 else "ldx1"
                cnt.inc(nc.gpsimd.dma_start(
                    xbB[c % 2][:, :].rearrange("p (a d) -> p a d", a=MT),
                    xin[c * C:(c + 1) * C, :].rearrange("(a p) d -> p a d", p=P)),
                    sm, 16)
                LT[("bx", c)] = (sm, cnt.val[sm])

            def load_Sc(c):
                sm = "lds0" if c % 2 == 0 else "lds1"
                cnt.inc(nc.gpsimd.dma_start(
                    Sc[c % 2][:, :], sp_S[c * P:(c + 1) * P, :]), sm, 16)
                LT[("sc", c)] = (sm, cnt.val[sm])

            def load_small(c):
                cnt.inc(nc.gpsimd.dma_start(
                    qTs_l[c % 2][:, :], sp_qTs[c * P:(c + 1) * P, :]), "ldsm", 16)
                ldv[c] = cnt.val["ldsm"]

            def load_at(c):
                cnt.inc(nc.gpsimd.dma_start(
                    at_l[:, :], sp_at[c * P:(c + 1) * P, :]), "ldat", 16)
                cnt.inc(nc.gpsimd.dma_start(
                    vt_l[:, :], sp_vt[c * P:(c + 1) * P, :]), "ldat", 16)
                LT[("at", c)] = ("ldat", cnt.val["ldat"])

            load_xB(0)
            load_Sc(0)
            load_small(0)
            load_at(0)
            v_g0p = 0
            pe_rd = [0] * (NC2 + 1)
            pe_grp = [[0] * 4 for _ in range(NC2 + 1)]
            s_rd = [0] * (NC2 + 1)
            st_done = [0, 0]
            if nch_b > 1:
                load_xB(1)
                load_Sc(1)
                load_small(1)

            def emit_readT(c):
                vwait("ldsm", ldv[c])
                cnt.inc(nc.vector.tensor_scalar(
                    qTg[:, :], qTs_l[c % 2][:, :], lamcc[:, c:c + 1], None,
                    A.mult), "v")
                v_qtg = cnt.val["v"]
                twait("ldsm", ldv[c])
                twait(*LT[("sc", c)])
                twait(*LT[("at", c)])
                twait("v", v_qtg)
                for gi, (bank, bname) in enumerate(
                        [(psA, "A"), (psB, "B"), (psC, "C"), (psD, "D")]):
                    acquire(bname)
                    for half in range(2):
                        j = gi * 2 + half
                        tgt = bank[:, half * C:(half + 1) * C]
                        for a in range(MT):
                            mm = nc.tensor.matmul(
                                tgt,
                                vt_l[:, a * D + j * P: a * D + (j + 1) * P],
                                at_l[:, a * C:(a + 1) * C],
                                start=(a == 0), stop=False)
                        for kk in range(KT):
                            mm = nc.tensor.matmul(
                                tgt, Sc[c % 2][:, kk * D + j * P: kk * D + (j + 1) * P],
                                qTs_l[c % 2][:, kk * C:(kk + 1) * C],
                                start=False, stop=False)
                        for kk in range(KT):
                            mm = nc.tensor.matmul(
                                tgt, M_r[:, kk * D + j * P: kk * D + (j + 1) * P],
                                qTg[:, kk * C:(kk + 1) * C],
                                start=False, stop=(kk == KT - 1))
                    cnt.inc(mm, "pe")
                    pe_grp[c][gi] = cnt.val["pe"]
                pe_rd[c] = cnt.val["pe"]

            def emit_rd_evacs(c):
                for gi, (bank, bname) in enumerate(
                        [(psA, "A"), (psB, "B"), (psC, "C"), (psD, "D")]):
                    swait("pe", pe_grp[c][gi])
                    for half in range(2):
                        j = gi * 2 + half
                        cnt.inc(nc.scalar.activation(
                            rdT[:, j * C:(j + 1) * C],
                            bank[:, half * C:(half + 1) * C], AF.Copy), "s")
                    release(bname, "s")
                s_rd[c] = cnt.val["s"]

            emit_readT(0)
            emit_rd_evacs(0)

            for c in range(nch_b):
                if c + 2 < nch_b:
                    cnt.wait(nc.gpsimd, "g", "pe", pe_rd[c])
                    load_small(c + 2)
                if c + 1 < nch_b:
                    cnt.wait(nc.gpsimd, "g", "pe", pe_rd[c])
                    load_at(c + 1)
                xc = xbB[c % 2]

                twait("s", s_rd[c])
                wb = [(psF, "F"), (psE, "E")]
                pe_wo = [0, 0]
                for mt in range(MT):
                    bank, bname = wb[mt]
                    acquire(bname)
                    for h in range(2):
                        for kk in range(KT):
                            mm = nc.tensor.matmul(
                                bank[:, h * 512:(h + 1) * 512],
                                rdT[:, kk * C + mt * P: kk * C + (mt + 1) * P],
                                ws["wo"][:, kk * D + h * 512: kk * D + (h + 1) * 512],
                                start=(kk == 0), stop=False)
                        mm = nc.tensor.matmul(
                            bank[:, h * 512:(h + 1) * 512], ones_row[64:65, :P],
                            brows[64:65, h * 512:(h + 1) * 512],
                            start=False, stop=True)
                    cnt.inc(mm, "pe")
                    pe_wo[mt] = cnt.val["pe"]
                for mt in range(MT):
                    swait("pe", pe_wo[mt])
                    swait("v", v_g0p)
                    cnt.inc(nc.scalar.activation(
                        G0[:, mt * D:(mt + 1) * D], wb[mt][0][:, :], AF.Identity,
                        accum_out=STt[:, S_SUM + mt:S_SUM + mt + 1]), "s")
                    release(wb[mt][1], "s")
                vwait("s")
                ln_tail(G0[:, 0:D], G0[:, D:2 * D], xqb, G1)
                vwait("s")
                vwait(*LT[("bx", c)])
                for mt in range(MT):
                    cnt.inc(nc.vector.scalar_tensor_tensor(
                        xc[:, mt * D:(mt + 1) * D], G1[:, mt * D:(mt + 1) * D],
                        STt[:, S_INV + mt:S_INV + mt + 1],
                        xc[:, mt * D:(mt + 1) * D], A.mult, A.add,
                        accum_out=STt[:, S_SUM + mt:S_SUM + mt + 1]), "v")

                if c + 1 < nch_b:
                    emit_readT(c + 1)

                ln_tail(xc[:, 0:D], xc[:, D:2 * D], xqb, G1)
                vwait("s")
                for mt in range(MT):
                    cnt.inc(nc.vector.tensor_scalar(
                        G0[:, mt * D:(mt + 1) * D], G1[:, mt * D:(mt + 1) * D],
                        STt[:, S_INV + mt:S_INV + mt + 1], None, A.mult), "v")
                for mt in range(MT):
                    cnt.inc(nc.vector.tensor_reduce(
                        STt[:, S_XS + mt:S_XS + mt + 1],
                        G0[:, mt * D:(mt + 1) * D],
                        AX.X, A.max, apply_absolute_value=True), "v")
                vflush()
                cnt.inc(nc.vector.tensor_scalar(
                    STt[:, S_XS:S_XS + 2], STt[:, S_XS:S_XS + 2], 1e-8, QMAX_INV,
                    A.max, A.mult), "v")
                vflush()
                cnt.inc(nc.vector.reciprocal(
                    STt[:, S_REC:S_REC + 2], STt[:, S_XS:S_XS + 2]), "v")
                vflush()
                tr_base = cnt.val["trq"]
                vwait("trq", tr_base)
                pe_ff = [0, 0]
                for mt in range(MT):
                    for h in range(2):
                        cnt.inc(nc.vector.tensor_scalar(
                            I1[:, :], G0[:, mt * D + h * 512: mt * D + (h + 1) * 512],
                            STt[:, S_REC + mt:S_REC + mt + 1], None, A.mult), "v")
                        cnt.inc(nc.vector.tensor_copy(
                            xqb[:, mt * D + h * 512: mt * D + (h + 1) * 512],
                            I1[:, :]), "v")
                    v_xq = cnt.val["v"]
                    ywait("v", v_xq)
                    out_ap = bass.AP(xqT, mt * D, [[MT * D, P], [P, KT], [1, P]])
                    cnt.inc(nc.sync.dma_start_transpose(
                        out_ap, xqb[:, mt * D:(mt + 1) * D]), "trq", 16)
                for mt in range(MT):
                    bank, bname = wb[mt]
                    twait("trq", tr_base + 16 * (mt + 1))
                    acquire(bname)
                    for h in range(2):
                        for kk in range(KT):
                            mm = nc.tensor.matmul(
                                bank[:, h * 512:(h + 1) * 512],
                                xqT[:, mt * D + kk * P: mt * D + (kk + 1) * P],
                                wfT[:, kk * D + h * 512: kk * D + (h + 1) * 512],
                                start=(kk == 0), stop=(kk == KT - 1))
                    cnt.inc(mm, "pe")
                    pe_ff[mt] = cnt.val["pe"]
                for mt in range(MT):
                    vwait("pe", pe_ff[mt])
                    cnt.inc(nc.vector.scalar_tensor_tensor(
                        G1[:, mt * D:(mt + 1) * D], wb[mt][0][:, :],
                        STt[:, S_XS + mt:S_XS + mt + 1], wsc_bc[:, :],
                        A.mult, A.mult,
                        accum_out=STt[:, S_SUM + mt:S_SUM + mt + 1]), "v")
                    release(wb[mt][1], "v")
                    cnt.inc(nc.vector.tensor_tensor(
                        G0[:, mt * D:(mt + 1) * D], G1[:, mt * D:(mt + 1) * D],
                        bf_bc[:, :], A.add), "v")
                ln_tail(G0[:, 0:D], G0[:, D:2 * D], xqb, G1, add_sbf=True)
                v_g0p = cnt.val["v"]
                vwait("s")
                for mt in range(MT):
                    cnt.inc(nc.vector.scalar_tensor_tensor(
                        xc[:, mt * D:(mt + 1) * D], G1[:, mt * D:(mt + 1) * D],
                        STt[:, S_INV + mt:S_INV + mt + 1],
                        xc[:, mt * D:(mt + 1) * D], A.mult, A.add), "v")
                v_out = cnt.val["v"]
                gwait("v", v_out)
                for mt in range(MT):
                    cnt.inc(nc.gpsimd.dma_start(
                        yout[c * C + mt * P: c * C + (mt + 1) * P, :],
                        xc[:, mt * D:(mt + 1) * D]),
                        "st0" if c % 2 == 0 else "st1", 16)
                st_done[c % 2] = cnt.val["st0" if c % 2 == 0 else "st1"]

                if c + 1 < nch_b:
                    emit_rd_evacs(c + 1)

                if c + 2 < nch_b:
                    cnt.wait(nc.gpsimd, "g", "st" + str(c % 2), st_done[c % 2])
                    load_xB(c + 2)
                    load_Sc(c + 2)

            if taps:
                local = {"qTs_l": qTs_l[(nch_b - 1) % 2],
                         "at_l": at_l,
                         "vt_l": vt_l,
                         "Sc0": Sc[(nch_b - 1) % 2], "rdT": rdT, "M_r": M_r,
                         "qTg": qTg, "xb0": xbB[0], "xb1": xbB[1]}
                ywait("v")
                ywait("s")
                for nm, ap_h in dbg.items():
                    if nm not in local:
                        continue
                    src = local[nm]
                    wtot = ap_h.shape[1]
                    for off in range(0, wtot, D):
                        wd = min(D, wtot - off)
                        ywait("v")
                        cnt.inc(nc.vector.tensor_copy(G0[:, :wd],
                                                      src[:, off:off + wd]), "v")
                        ywait("v")
                        cnt.inc(nc.sync.dma_start(ap_h[:, off:off + wd],
                                                  G0[:, :wd]), "st", 16)
                        ywait("st")
                        cnt.wait(nc.vector, "v", "st", cnt.val["st"])
            nc.sync.wait_ge(cnt.st0, cnt.val["st0"])
            if cnt.val["st1"]:
                nc.sync.wait_ge(cnt.st1, cnt.val["st1"])
            if cnt.val["st"]:
                nc.sync.wait_ge(cnt.st, cnt.val["st"])
    return nc


def kernel(L0_attn_x, L0_attn_Wq, L0_attn_bq, L0_attn_Wk, L0_attn_bk,
           L0_attn_Wv, L0_attn_bv, L0_attn_Wo, L0_attn_bo,
           L0_attn_decay_param, L0_ffn_weight, L0_ffn_bias):
    x = np.ascontiguousarray(np.asarray(L0_attn_x, dtype=np.float32))
    B = x.shape[0]
    iota_bc, iota_rev, expo, binm, ident = _consts()
    iota_cc = np.broadcast_to(
        (np.arange(NC2, dtype=np.float32) * C)[None, :], (P, NC2)).copy()
    common = {
        "wq": np.ascontiguousarray(L0_attn_Wq, dtype=np.float32),
        "wk": np.ascontiguousarray(L0_attn_Wk, dtype=np.float32),
        "wv": np.ascontiguousarray(L0_attn_Wv, dtype=np.float32),
        "wo": np.ascontiguousarray(L0_attn_Wo, dtype=np.float32),
        "wf": np.ascontiguousarray(L0_ffn_weight, dtype=np.float32),
        "bq": np.ascontiguousarray(L0_attn_bq, dtype=np.float32),
        "bk": np.ascontiguousarray(L0_attn_bk, dtype=np.float32),
        "bv": np.ascontiguousarray(L0_attn_bv, dtype=np.float32),
        "bo": np.ascontiguousarray(L0_attn_bo, dtype=np.float32),
        "bf": np.ascontiguousarray(L0_ffn_bias, dtype=np.float32),
        "decay": np.asarray(L0_attn_decay_param, dtype=np.float32).reshape(1, 1),
        "iota_bc": iota_bc, "iota_rev": iota_rev,
        "expo": expo, "binm": binm, "ident": ident, "iota_cc": iota_cc,
    }
    nc = build_nc2(nb=B)
    in_maps = []
    for core in range(2 * B):
        b, h = core // 2, core % 2
        m = dict(common,
                 xin=np.ascontiguousarray(x[b, h * TH:(h + 1) * TH]),
                 rflag=np.full((P, 1), float(h), np.float32),
                 sflag=np.full((P, 1), float(1 - h), np.float32))
        in_maps.append(m)
    res = run_bass_kernel_spmd(nc, in_maps, core_ids=list(range(2 * B)))
    out = np.empty((B, T, D), np.float32)
    for core in range(2 * B):
        b, h = core // 2, core % 2
        out[b, h * TH:(h + 1) * TH] = np.asarray(res.results[core]["yout"])
    return out
